# version 13 (final)
# speedup vs baseline: 1.0275x; 1.0033x over previous
import numpy as np
from contextlib import ExitStack

import concourse.bass as bass
import concourse.mybir as mybir
import bass_rust

F32 = mybir.dt.float32
BF16 = mybir.dt.bfloat16
I8 = mybir.dt.int8
AF = mybir.ActivationFunctionType
ALU = mybir.AluOpType
AXX = mybir.AxisListType.X

NTOK = 2048
D = 1024
NC_TOK = NTOK // 128
NC_D = D // 128
DH = 4096
NC_H = DH // 128
FBLK = 512
DBG_INTRA_ONLY = False
DBG_VMT = True
DBG_TAP = False
NBLK = NTOK // FBLK


class Sem:
    def __init__(self, stack, nc, name):
        self.h = stack.enter_context(nc.semaphore(name))
        self.n = 0

    def inc(self, k=1):
        self.n += k
        return self.n


class SB:
    def __init__(self, s):
        self.s, self.b = s, s.n
    @property
    def h(self):
        return self.s.h
    @property
    def n(self):
        return self.s.n
    def inc(self, k=1):
        return self.s.inc(k)
    def w(self, x):
        return self.b + x


class Prog:
    def __init__(self):
        self.nc = bass.Bass("TRN2", target_bir_lowering=False)
        self.stack = ExitStack()
        self.sems = {}
        self.host_consts = {}
        self._uid = 0

    def uid(self, p):
        self._uid += 1
        return f"{p}{self._uid}"

    def sem(self, name):
        if name not in self.sems:
            self.sems[name] = Sem(self.stack, self.nc, name)
        return self.sems[name]

    def sbuf(self, name, shape, dt):
        return self.stack.enter_context(self.nc.sbuf_tensor(name, list(shape), dt))

    def psum(self, name, shape, dt=F32):
        return self.stack.enter_context(self.nc.psum_tensor(name, list(shape), dt))

    def dram_in(self, name, shape, dt=F32):
        return self.nc.dram_tensor(name, list(shape), dt, kind="ExternalInput").ap()

    def dram_out(self, name, shape, dt=F32):
        return self.nc.dram_tensor(name, list(shape), dt, kind="ExternalOutput").ap()

    def dram_tmp(self, name, shape, dt=F32):
        return self.nc.dram_tensor(name, list(shape), dt, kind="Internal").ap()

    def const_in(self, name, arr):
        arr = np.ascontiguousarray(arr)
        self.host_consts[name] = arr
        dtmap = {np.dtype(np.float32): F32, np.dtype(np.int8): I8}
        return self.dram_in(name, arr.shape, dtmap[arr.dtype])


class Kern:
    def __init__(self, P: Prog):
        self.P = P
        self.resid = P.sbuf("resid", [128, NC_TOK * D], F32)
        self.stats = P.sbuf("stats", [128, 16 * 8], F32)
        self.eye_bf = P.sbuf("eye_bf", [128, 128], BF16)
        self.ones_f32 = P.sbuf("ones_f32", [128, 1], F32)
        self.eps_col = P.sbuf("eps_col", [128, 1], F32)
        self.sq_scr = P.sbuf("sq_scr", [128, 1024], F32)
        self.eye_in = P.const_in("c_eye", np.eye(128, dtype=np.float32))
        eexp = np.zeros((16, 1024), np.float32)
        for j in range(8):
            eexp[2 * j, j * 128:j * 128 + 64] = 1.0
            eexp[2 * j + 1, j * 128 + 64:(j + 1) * 128] = 1.0
        self.eexp_in = P.const_in("c_eexp", eexp)
        tri = (np.arange(128)[None, :] >= np.arange(128)[:, None]).astype(np.float32)
        self.tri_in = P.const_in("c_tri", tri)
        self.tri_f32 = P.sbuf("tri_f32", [128, 128], F32)
        self.tri_bf = P.sbuf("tri_bf", [128, 128], BF16)
        self.eye_f32 = P.sbuf("eye_f32", [128, 128], F32)
        self.xn = P.sbuf("xn", [128, NC_TOK * D], BF16)
        self.xT = P.sbuf("xT", [128, NC_D * NTOK], BF16)
        self.scr_d = P.dram_tmp("scr_d", [8192], F32)

    def emit_init(self, x_in):
        P, nc = self.P, self.P.nc
        s_ld = P.sem("ini_ld")
        with nc.Block("init") as blk:
            @blk.sync
            def _(s):
                xap = bass_rust.AP(tensor=x_in.tensor, offset=0,
                                   ap=[[1024, 128], [128 * 1024, 16], [1, 1024]])
                s.dma_start(out=self.resid[:], in_=xap).then_inc(s_ld.h, 16) ; s_ld.inc(16)
                s.dma_start(out=self.eye_f32[:], in_=self.eye_in[:]).then_inc(s_ld.h, 16) ; s_ld.inc(16)
                s.dma_start(out=self.tri_f32[:], in_=self.tri_in[:]).then_inc(s_ld.h, 16) ; s_ld.inc(16)
            @blk.vector
            def _(v):
                v.memset(self.ones_f32[:], 1.0)
                v.memset(self.eps_col[:], 1e-5)
                v.wait_ge(s_ld.h, s_ld.n)
                v.tensor_copy(self.eye_bf[:], self.eye_f32[:])
                v.tensor_copy(self.tri_bf[:], self.tri_f32[:])
                v.drain()

    def emit_fin(self, out_ap):
        P, nc = self.P, self.P.nc
        s_st = P.sem("fin_st")
        with nc.Block("fin") as blk:
            @blk.sync
            def _(s):
                oap = bass_rust.AP(tensor=out_ap.tensor, offset=0,
                                    ap=[[1024, 128], [128 * 1024, 16], [1, 1024]])
                s.dma_start(out=oap, in_=self.resid[:]).then_inc(s_st.h, 16) ; s_st.inc(16)
                s.wait_ge(s_st.h, s_st.n)

    def emit_ln(self, blk, tag, wait=()):
        P = self.P
        st = self.stats
        s_sum = P.sem(f"{tag}_lnsum")
        s_sq = P.sem(f"{tag}_lnsq")
        s_stat = P.sem(f"{tag}_lnstat")
        s_app = P.sem(f"{tag}_lnapp")
        SUM, SQ, MM, VAR, RR, NMR = 0, 16, 32, 48, 64, 80

        @blk.vector
        def _(v):
            for s_, v_ in wait:
                v.wait_ge(s_.h, v_)
            for c in range(NC_TOK):
                ins = v.reduce_sum(st[:, SUM + c:SUM + c + 1], self.resid[:, c * D:(c + 1) * D], axis=AXX)
            ins.then_inc(s_sum.h, 1) ; s_sum.inc(1)

        @blk.scalar
        def _(s):
            for s_, v_ in wait:
                s.wait_ge(s_.h, v_)
            for c in range(NC_TOK):
                s.activation(self.sq_scr[:], self.resid[:, c * D:(c + 1) * D], AF.Square,
                             accum_out=st[:, SQ + c:SQ + c + 1])
                s.drain()
            s.nop().then_inc(s_sq.h, 1) ; s_sq.inc(1)

        @blk.vector
        def _(v):
            v.wait_ge(s_sum.h, s_sum.n)
            v.wait_ge(s_sq.h, s_sq.n)
            v.tensor_scalar(st[:, MM:MM + 16], st[:, SUM:SUM + 16], 1.0 / D, None, ALU.mult)
            v.tensor_scalar(st[:, VAR:VAR + 16], st[:, SQ:SQ + 16], 1.0 / D, None, ALU.mult)
            v.drain()
            v.tensor_tensor(self.sq_scr[:, 0:16], st[:, MM:MM + 16], st[:, MM:MM + 16], ALU.mult)
            v.drain()
            v.tensor_tensor(st[:, VAR:VAR + 16], st[:, VAR:VAR + 16], self.sq_scr[:, 0:16],
                            ALU.subtract).then_inc(s_stat.h, 1) ; s_stat.inc(1)

        @blk.scalar
        def _(s):
            s.wait_ge(s_stat.h, s_stat.n)
            s.activation(st[:, VAR:VAR + 16], st[:, VAR:VAR + 16], AF.Ln, bias=self.eps_col[:])
            s.drain()
            s.activation(st[:, RR:RR + 16], st[:, VAR:VAR + 16], AF.Exp,
                         scale=-0.5).then_inc(s_stat.h, 1) ; s_stat.inc(1)

        @blk.vector
        def _(v):
            v.wait_ge(s_stat.h, s_stat.n)
            v.scalar_tensor_tensor(st[:, NMR:NMR + 16], st[:, MM:MM + 16], -1.0,
                                   st[:, RR:RR + 16], ALU.mult,
                                   ALU.mult).then_inc(s_stat.h, 1) ; s_stat.inc(1)

        @blk.scalar
        def _(s):
            s.wait_ge(s_stat.h, s_stat.n)
            for c in range(NC_TOK):
                ins = s.activation(self.xn[:, c * D:(c + 1) * D], self.resid[:, c * D:(c + 1) * D],
                                   AF.Identity, scale=st[:, RR + c:RR + c + 1],
                                   bias=st[:, NMR + c:NMR + c + 1])
            ins.then_inc(s_app.h, 1) ; s_app.inc(1)
        return s_app

    def emit_transpose(self, blk, src_tm, dst_fm, ncd, tp_ps, wait, tag, src_w=None):
        P = self.P
        W = src_w if src_w is not None else ncd * 128
        s_tp = P.sem(f"{tag}_tp")
        s_ev = P.sem(f"{tag}_tpev")
        groups = [(f, g) for f in range(ncd) for g in range(2)]
        ev_base = s_ev.n
        tp_base = s_tp.n

        @blk.tensor
        def _(t):
            for s_, v_ in wait:
                t.wait_ge(s_.h, v_)
            for gi, (f, g) in enumerate(groups):
                ps = tp_ps[:, (gi % 2) * 1024:(gi % 2) * 1024 + 1024]
                if gi >= 2:
                    t.wait_ge(s_ev.h, ev_base + gi - 1)
                for i in range(8):
                    c = g * 8 + i
                    ins = t.transpose(ps[:, i * 128:(i + 1) * 128],
                                      src_tm[:, c * W + f * 128: c * W + f * 128 + 128],
                                      self.eye_bf[:])
                ins.then_inc(s_tp.h, 1)
            s_tp.n += len(groups)

        @blk.scalar
        def _(s):
            for gi, (f, g) in enumerate(groups):
                ps = tp_ps[:, (gi % 2) * 1024:(gi % 2) * 1024 + 1024]
                s.wait_ge(s_tp.h, tp_base + gi + 1)
                s.activation(dst_fm[:, f * NTOK + g * 1024: f * NTOK + (g + 1) * 1024], ps[:],
                             AF.Identity).then_inc(s_ev.h, 1)
            s_ev.n += len(groups)
        return s_ev

    def emit_wprep_ffn(self, lname, w1b_in, w2t_in, convw_in, convb_in):
        P, nc = self.P, self.P.nc
        tag = f"wp{lname}"
        fb = {
            "w1q": P.dram_tmp(f"{tag}_w1q", [32, 1024, 128], BF16),
            "w2q": P.dram_tmp(f"{tag}_w2q", [4096, 1024], BF16),
            "convw": P.sbuf(f"{tag}_convw", [128, 96], F32),
            "convb": P.sbuf(f"{tag}_convb", [128, 32], F32),
            "sc": P.sbuf(f"{tag}_sc", [128, 4], F32),
        }
        s_lde = P.sem("wp_lde")
        s_ldo = P.sem("wp_ldo")
        s_ld = P.sem("wp_ld")
        s_red = P.sem("wp_red")
        s_tot = P.sem("wp_tot")
        s_sc = P.sem("wp_sc")
        s_bc = P.sem("wp_bc")
        s_q = P.sem("wp_q")
        s_st = P.sem("wp_st")
        WB = {s: s.n for s in (s_lde, s_ldo, s_ld, s_red, s_tot, s_sc, s_bc, s_q, s_st)}
        CH = 4096
        NCHK = 16

        def _w1ap(t, i):
            return bass_rust.AP(tensor=t, offset=4 * i * 131072,
                                ap=[[128, 128], [131072, 4], [16384, 8], [1, 128]])

        def _w2ap(t, j):
            return bass_rust.AP(tensor=t, offset=512 * j * 1024,
                                ap=[[1024, 128], [131072, 4], [1, 1024]])

        def src_ap(i):
            return _w1ap(w1b_in.tensor, i) if i < 8 else _w2ap(w2t_in.tensor, i - 8)

        def dst_ap(i):
            return _w1ap(fb["w1q"].tensor, i) if i < 8 else _w2ap(fb["w2q"].tensor, i - 8)

        with nc.sbuf_tensor(P.uid("wsrc"), [128, 2 * CH], F32) as wsrc, \
             nc.sbuf_tensor(P.uid("wq8"), [128, CH], I8) as wq8, \
             nc.sbuf_tensor(P.uid("wqb"), [128, CH], BF16) as wqb, \
             nc.sbuf_tensor(P.uid("wtf"), [128, CH], F32) as wtf, \
             nc.sbuf_tensor(P.uid("wacc"), [128, 4], F32) as wacc, \
             nc.sbuf_tensor(P.uid("wrow"), [1, 8], F32) as wrow, \
             nc.psum_tensor(P.uid("wtot"), [1, 2], F32) as ptot:

            with nc.Block(f"{tag}A") as blk:
                @blk.sync
                def _(s):
                    for i in range(NCHK):
                        par = (s_lde, s_ldo)[i % 2]
                        if i >= 2:
                            s.wait_ge(s_red.h, WB[s_red] + 2 * (i - 1))
                        s.dma_start(out=wsrc[:, (i % 2) * CH:(i % 2) * CH + CH],
                                    in_=src_ap(i)).then_inc(par.h, 16) ; par.inc(16)
                    s.dma_start(out=fb["convw"][:], in_=convw_in[:]).then_inc(s_ld.h, 16) ; s_ld.inc(16)
                    s.wait_ge(s_ld.h, s_ld.n)
                    s.dma_start(out=fb["convb"][:], in_=convb_in[:]).then_inc(s_ld.h, 16) ; s_ld.inc(16)
                    s.wait_ge(s_ld.h, s_ld.n)
                    s.wait_ge(s_sc.h, WB[s_sc] + 1)
                    s.dma_start(out=self.scr_d[7168:7172], in_=wrow[0:1, 2:6]).then_inc(s_bc.h, 16) ; s_bc.inc(16)
                    s.wait_ge(s_bc.h, s_bc.n)
                    bc = bass_rust.AP(tensor=self.scr_d.tensor, offset=7168, ap=[[0, 128], [1, 4]])
                    s.dma_start(out=fb["sc"][:], in_=bc).then_inc(s_bc.h, 16) ; s_bc.inc(16)
                    s.wait_ge(s_bc.h, s_bc.n)

                @blk.vector
                def _(g):
                    g.memset(wacc[:, 0:2], 0.0)
                    for i in range(NCHK):
                        g.wait_ge((s_lde, s_ldo)[i % 2].h, WB[(s_lde, s_ldo)[i % 2]] + 16 * (i // 2 + 1))
                        slot = 0 if i < 8 else 1
                        buf = wsrc[:, (i % 2) * CH:(i % 2) * CH + CH]
                        g.tensor_reduce(wacc[:, 2 + slot:3 + slot], buf, axis=AXX, op=ALU.add,
                                        apply_absolute_value=True)
                        g.drain()
                        g.tensor_tensor(wacc[:, slot:slot + 1], wacc[:, slot:slot + 1],
                                        wacc[:, 2 + slot:3 + slot], ALU.add)
                        g.drain()
                        g.nop().then_inc(s_red.h, 2)
                    s_red.n += 2 * NCHK

                @blk.tensor
                def _(t):
                    t.wait_ge(s_red.h, WB[s_red] + 2 * NCHK)
                    t.matmul(ptot[:], self.ones_f32[:], wacc[:, 0:2], start=True,
                             stop=True).then_inc(s_tot.h, 1) ; s_tot.inc(1)

                @blk.vector
                def _(v):
                    v.wait_ge(s_tot.h, s_tot.n)
                    v.tensor_scalar(wrow[:, 0:2], ptot[:], 1.0 / (4096.0 * 1024.0), 1e-5,
                                    ALU.mult, ALU.max)
                    v.drain()
                    v.reciprocal(wrow[:, 2:4], wrow[:, 0:2])
                    v.drain()
                    v.tensor_scalar(wrow[:, 4:6], wrow[:, 0:2], 1.0 / 127.0,
                                    None, ALU.mult).then_inc(s_sc.h, 1) ; s_sc.inc(1)

            with nc.Block(f"{tag}B") as blk:
                @blk.sync
                def _(s):
                    for i in range(NCHK):
                        par = (s_lde, s_ldo)[i % 2]
                        if i >= 2:
                            s.wait_ge(s_q.h, WB[s_q] + i - 1)
                        s.dma_start(out=wsrc[:, (i % 2) * CH:(i % 2) * CH + CH],
                                    in_=src_ap(i)).then_inc(par.h, 16) ; par.inc(16)
                    s.wait_ge(s_st.h, WB[s_st] + 16 * NCHK)

                @blk.gpsimd
                def _(g):
                    for i in range(NCHK):
                        g.wait_ge(s_q.h, WB[s_q] + i + 1)
                        g.dma_start(out=dst_ap(i), in_=wqb[:]).then_inc(s_st.h, 16) ; s_st.inc(16)
                        g.wait_ge(s_st.h, s_st.n)

                @blk.vector
                def _(v):
                    base_e = s_lde.n - 16 * 8
                    base_o = s_ldo.n - 16 * 8
                    for i in range(NCHK):
                        v.wait_ge((s_lde, s_ldo)[i % 2].h,
                                  (base_e, base_o)[i % 2] + 16 * (i // 2 + 1))
                        slot = 0 if i < 8 else 1
                        buf = wsrc[:, (i % 2) * CH:(i % 2) * CH + CH]
                        if i > 0:
                            v.wait_ge(s_st.h, WB[s_st] + 16 * i)
                        v.tensor_scalar(wtf[:], buf, fb["sc"][:, slot:slot + 1], 1.0,
                                        ALU.mult, ALU.min)
                        v.drain()
                        v.tensor_scalar(wq8[:], wtf[:], -1.0, None, ALU.max)
                        v.drain()
                        v.tensor_copy(wqb[:], wq8[:])
                        v.drain()
                        v.nop().then_inc(s_q.h, 1)
                    s_q.n += NCHK
        return fb

    def emit_ffn(self, tag, fb):
        P, nc = self.P, self.P.nc
        sc = fb["sc"]
        QCH = 4
        NQ = NC_H // QCH
        S = {k: P.sem(f"ffn_{k}") for k in
             ["q1", "iv", "ivv", "ivt", "w1le", "w1lo", "w1c", "m1", "hd", "si", "ax", "axg",
              "s2t", "s2v", "s2d", "q2", "w2le", "w2lo", "w2c", "m2", "m2q", "ev2"]}
        B = {k: s.n for k, s in S.items()}
        hs = self.xn
        n1T = self.xT

        with nc.Block(f"{tag}") as blk, \
             nc.sbuf_tensor(P.uid("qcols"), [128, 80], F32) as qcols, \
             nc.sbuf_tensor(P.uid("i8t"), [128, 4096], I8) as i8t, \
             nc.sbuf_tensor(P.uid("w1b"), [128, 2 * 1024], BF16) as w1b, \
             nc.sbuf_tensor(P.uid("w2b"), [128, 2 * QCH * 1024], BF16) as w2b, \
             nc.sbuf_tensor(P.uid("hdt"), [128, 2 * 514], F32) as hdt, \
             nc.sbuf_tensor(P.uid("ct"), [128, 2 * 512], F32) as ct, \
             nc.sbuf_tensor(P.uid("haloh"), [128, 96], F32) as haloh, \
             nc.sbuf_tensor(P.uid("hdsv"), [128, 32], F32) as hdsv, \
             nc.sbuf_tensor(P.uid("axacc"), [128, 4 * 512], BF16) as axacc, \
             nc.sbuf_tensor(P.uid("axg"), [128, 4 * 512], BF16) as axg, \
             nc.sbuf_tensor(P.uid("axf"), [128, 512], F32) as axf, \
             nc.sbuf_tensor(P.uid("s2b"), [128, 2048], F32) as s2b, \
             nc.sbuf_tensor(P.uid("inv1b"), [128, 2048], F32) as inv1b, \
             nc.sbuf_tensor(P.uid("ivrow"), [16, 128], F32) as ivrow, \
             nc.sbuf_tensor(P.uid("s2row"), [4, 128], F32) as s2row:

            AM, S1, IV1, AM2, S2C, IV2 = 0, 16, 32, 48, 52, 56
            s_app = self.emit_ln(blk, "ffnln")

            @blk.vector
            def _(v):
                v.wait_ge(s_app.h, s_app.n)
                for c in range(NC_TOK):
                    v.reduce_max(qcols[:, AM + c:AM + c + 1], self.xn[:, c * D:(c + 1) * D],
                                 axis=AXX, apply_absolute_value=True)
                v.drain()
                v.tensor_scalar(qcols[:, AM:AM + 16], qcols[:, AM:AM + 16], 1e-5, None, ALU.max)
                v.drain()
                v.reciprocal(qcols[:, S1:S1 + 16], qcols[:, AM:AM + 16])
                v.drain()
                v.tensor_scalar(qcols[:, S1:S1 + 16], qcols[:, S1:S1 + 16], 127.0, None, ALU.mult)
                v.tensor_scalar(qcols[:, IV1:IV1 + 16], qcols[:, AM:AM + 16], sc[:, 2:3], None, ALU.mult)
                v.drain()
                for cg in range(NC_TOK // 4):
                    for c4 in range(4):
                        c = cg * 4 + c4
                        v.tensor_scalar(i8t[:, c4 * 1024:(c4 + 1) * 1024], self.xn[:, c * D:(c + 1) * D],
                                        qcols[:, S1 + c:S1 + c + 1], None, ALU.mult)
                    v.drain()
                    for c4 in range(4):
                        c = cg * 4 + c4
                        v.tensor_copy(self.xn[:, c * D:(c + 1) * D], i8t[:, c4 * 1024:(c4 + 1) * 1024])
                    v.drain()
                v.nop().then_inc(S["q1"].h, 1) ; S["q1"].inc(1)

            with nc.psum_tensor(P.uid("tpp"), [128, 2048], BF16) as tp_ps, \
                 nc.psum_tensor(P.uid("ivp"), [16, 128], F32) as ivp:
                s_t1 = self.emit_transpose(blk, self.xn, n1T, NC_D, tp_ps,
                                           [(S["q1"], S["q1"].n)], "ffnt1")

                @blk.tensor
                def _(t):
                    t.wait_ge(S["q1"].h, S["q1"].n)
                    t.transpose(ivp[:], qcols[:, IV1:IV1 + 16], self.eye_f32[:]).then_inc(
                        S["ivt"].h, 1)
                    S["ivt"].inc(1)

                @blk.vector
                def _(v):
                    v.wait_ge(S["ivt"].h, S["ivt"].n)
                    v.tensor_copy(ivrow[:], ivp[:]).then_inc(S["ivv"].h, 1) ; S["ivv"].inc(1)

            @blk.sync
            def _(s):
                s.wait_ge(S["ivv"].h, S["ivv"].n)
                s.dma_start(out=self.scr_d[0:2048], in_=ivrow[:]).then_inc(S["iv"].h, 16) ; S["iv"].inc(16)
                s.wait_ge(S["iv"].h, S["iv"].n)
                bc = bass_rust.AP(tensor=self.scr_d.tensor, offset=0, ap=[[0, 128], [1, 2048]])
                s.dma_start(out=inv1b[:], in_=bc).then_inc(S["iv"].h, 16) ; S["iv"].inc(16)

            @blk.sync
            def _(s):
                for b in range(NBLK):
                    for ch in range(NC_H):
                        idx = b * NC_H + ch
                        if idx >= 2:
                            s.wait_ge(S["m1"].h, B["m1"] + idx - 1)
                        w1ap = bass_rust.AP(tensor=fb["w1q"].tensor, offset=ch * 131072,
                                            ap=[[128, 128], [16384, 8], [1, 128]])
                        par = S["w1le"] if idx % 2 == 0 else S["w1lo"]
                        s.dma_start(out=w1b[:, (idx % 2) * 1024:(idx % 2) * 1024 + 1024],
                                    in_=w1ap).then_inc(par.h, 16) ; par.inc(16)
                    s.wait_ge(S["s2v"].h, B["s2v"] + 2 * (b + 1))
                    s.dma_start(out=self.scr_d[2048 + b * 512: 2048 + (b + 1) * 512],
                                in_=s2row[:]).then_inc(S["s2d"].h, 16) ; S["s2d"].inc(16)
                    s.wait_ge(S["s2d"].h, S["s2d"].n)
                    bc2 = bass_rust.AP(tensor=self.scr_d.tensor, offset=2048 + b * 512,
                                       ap=[[0, 128], [1, 512]])
                    for r4 in range(4):
                        s.dma_start(out=s2b[:, r4 * 512:(r4 + 1) * 512],
                                    in_=bc2).then_inc(S["s2d"].h, 16) ; S["s2d"].inc(16)
                    s.wait_ge(S["s2d"].h, S["s2d"].n)

            @blk.gpsimd
            def _(g):
                for b in range(NBLK):
                    for q in range(NQ):
                        idx = b * NQ + q
                        if idx >= 2:
                            qp, bp = (idx - 2) % NQ, (idx - 2) // NQ
                            if qp == NQ - 1:
                                g.wait_ge(S["m2"].h, B["m2"] + 4 * (bp + 1))
                            else:
                                g.wait_ge(S["m2q"].h, B["m2q"] + (NQ - 1) * bp + qp + 1)
                        w2ap = bass_rust.AP(tensor=fb["w2q"].tensor, offset=q * QCH * 128 * 1024,
                                            ap=[[1024, 128], [131072, QCH], [1, 1024]])
                        par = S["w2le"] if idx % 2 == 0 else S["w2lo"]
                        g.dma_start(out=w2b[:, (idx % 2) * QCH * 1024:((idx % 2) + 1) * QCH * 1024],
                                    in_=w2ap).then_inc(par.h, 16) ; par.inc(16)
                        g.wait_ge(par.h, par.n)


            with nc.psum_tensor(P.uid("pa"), [128, 4096], F32) as pa:
                HALO = 1024
                AXT = 1536

                @blk.tensor
                def _(t):
                    t.wait_ge(s_t1.h, s_t1.n)
                    t.wait_ge(S["iv"].h, S["iv"].n)
                    for b in range(NBLK):
                        for ch in range(NC_H):
                            idx = b * NC_H + ch
                            par1, pb1 = (S["w1le"], B["w1le"]) if idx % 2 == 0 else (S["w1lo"], B["w1lo"])
                            t.wait_ge(par1.h, pb1 + 16 * (idx // 2 + 1))
                            if idx >= 2:
                                t.wait_ge(S["hd"].h, B["hd"] + idx - 1)
                            if b >= 1 and ch <= 1:
                                t.wait_ge(S["ev2"].h, B["ev2"] + 4 * (b - 1) + 1)
                            ph = pa[:, (idx % 2) * 512:(idx % 2) * 512 + 512]
                            wb = w1b[:, (idx % 2) * 1024:(idx % 2) * 1024 + 1024]
                            for kf in range(NC_D):
                                ins = t.matmul(ph, wb[:, kf * 128:(kf + 1) * 128],
                                               n1T[:, kf * NTOK + b * FBLK: kf * NTOK + (b + 1) * FBLK],
                                               start=(kf == 0), stop=(kf == NC_D - 1))
                            if b == 0:
                                for kf in range(NC_D):
                                    hm = t.matmul(pa[:, HALO + ch * 3: HALO + ch * 3 + 3],
                                                  wb[:, kf * 128:(kf + 1) * 128],
                                                  n1T[:, kf * NTOK + FBLK: kf * NTOK + 3 * FBLK + 1: FBLK],
                                                  start=(kf == 0), stop=(kf == NC_D - 1))
                                hm.then_inc(S["m1"].h, 1) ; S["m1"].inc(1)
                            else:
                                ins.then_inc(S["m1"].h, 1) ; S["m1"].inc(1)
                        t.wait_ge(S["ax"].h, B["ax"] + b + 1)
                        if b >= 1:
                            t.wait_ge(S["ev2"].h, B["ev2"] + 4 * (b - 1) + 2)
                        for i in range(4):
                            ins = t.transpose(pa[:, AXT + i * 128: AXT + (i + 1) * 128],
                                              axf[:, i * 128:(i + 1) * 128], self.eye_f32[:])
                        ins.then_inc(S["s2t"].h, 1) ; S["s2t"].inc(1)
                        t.wait_ge(S["s2v"].h, B["s2v"] + 2 * b + 1)
                        t.transpose(pa[0:4, AXT:AXT + 128], qcols[:, S2C:S2C + 4],
                                    self.eye_f32[:]).then_inc(S["s2t"].h, 1) ; S["s2t"].inc(1)
                        for ch in range(NC_H):
                            t.wait_ge(S["q2"].h, B["q2"] + b * NC_H + (ch // 4 + 1) * 4)
                            qg = ch // QCH
                            gidx = b * NQ + qg
                            if ch % QCH == 0:
                                par2, pb2 = (S["w2le"], B["w2le"]) if gidx % 2 == 0 else (S["w2lo"], B["w2lo"])
                                t.wait_ge(par2.h, pb2 + 16 * (gidx // 2 + 1))
                            if ch == 0:
                                t.wait_ge(S["ev2"].h, B["ev2"] + 4 * b)
                            wslot = (gidx % 2) * QCH * 1024 + (ch % QCH) * 1024
                            for tc in range(4):
                                for half in range(2):
                                    ins = t.matmul(pa[:, tc * 1024 + half * 512: tc * 1024 + half * 512 + 512],
                                                   hs[:, ch * 512 + tc * 128: ch * 512 + tc * 128 + 128],
                                                   w2b[:, wslot + half * 512: wslot + half * 512 + 512],
                                                   start=(ch == 0), stop=(ch == NC_H - 1))
                                    if ch == NC_H - 1 and half == 1:
                                        ins.then_inc(S["m2"].h, 1) ; S["m2"].inc(1)
                                    elif ch % QCH == QCH - 1 and qg < NQ - 1 and tc == 3 and half == 1:
                                        ins.then_inc(S["m2q"].h, 1) ; S["m2q"].inc(1)

                @blk.vector
                def _(v):
                    v.wait_ge(S["iv"].h, S["iv"].n)
                    for b in range(NBLK):
                        for ch in range(NC_H):
                            idx = b * NC_H + ch
                            v.wait_ge(S["m1"].h, B["m1"] + idx + 1)
                            hd = hdt[:, (idx % 2) * 514:(idx % 2) * 514 + 514]
                            ph = pa[:, (idx % 2) * 512:(idx % 2) * 512 + 512]
                            v.tensor_tensor(hd[:, 1:513], ph, inv1b[:, b * FBLK:(b + 1) * FBLK], ALU.mult)
                            if b == 0:
                                v.tensor_tensor(haloh[:, ch * 3:ch * 3 + 3],
                                                pa[:, HALO + ch * 3:HALO + ch * 3 + 3],
                                                inv1b[:, FBLK:3 * FBLK + 1:FBLK], ALU.mult)
                                v.memset(hd[:, 0:1], 0.0)
                            else:
                                v.tensor_copy(hd[:, 0:1], hdsv[:, ch:ch + 1])
                            if b == NBLK - 1:
                                v.memset(hd[:, 513:514], 0.0)
                            elif b > 0:
                                v.tensor_copy(hd[:, 513:514], haloh[:, ch * 3 + b:ch * 3 + b + 1])
                            v.drain()
                            if b == 0:
                                v.tensor_copy(hd[:, 513:514], haloh[:, ch * 3:ch * 3 + 1])
                                v.drain()
                            cth = ct[:, (ch % 2) * 512:(ch % 2) * 512 + 512]
                            if idx >= 2:
                                v.wait_ge(S["si"].h, B["si"] + idx - 1)
                            v.tensor_copy(hdsv[:, ch:ch + 1], hd[:, 512:513])
                            v.tensor_scalar(cth, hd[:, 1:513], fb["convw"][:, ch * 3 + 1:ch * 3 + 2],
                                            fb["convb"][:, ch:ch + 1], ALU.mult, ALU.add)
                            v.drain()
                            v.scalar_tensor_tensor(cth, hd[:, 0:512], fb["convw"][:, ch * 3:ch * 3 + 1],
                                                   cth, ALU.mult, ALU.add)
                            v.drain()
                            ins = v.scalar_tensor_tensor(cth, hd[:, 2:514], fb["convw"][:, ch * 3 + 2:ch * 3 + 3],
                                                         cth, ALU.mult, ALU.add)
                            ins.then_inc(S["hd"].h, 1) ; S["hd"].inc(1)
                            if ch >= 1:
                                v.wait_ge(S["si"].h, B["si"] + idx)
                                pch = ch - 1
                                sl4 = (pch % 4) * 512
                                if pch < 4:
                                    v.tensor_copy(axacc[:, sl4:sl4 + 512], hs[:, pch * 512:(pch + 1) * 512])
                                    v.tensor_copy(axg[:, sl4:sl4 + 512], hs[:, pch * 512:(pch + 1) * 512])
                                else:
                                    v.tensor_tensor(axacc[:, sl4:sl4 + 512], axacc[:, sl4:sl4 + 512],
                                                    hs[:, pch * 512:(pch + 1) * 512], ALU.max)
                                    v.tensor_tensor(axg[:, sl4:sl4 + 512], axg[:, sl4:sl4 + 512],
                                                    hs[:, pch * 512:(pch + 1) * 512], ALU.min)
                                if pch % 4 == 3:
                                    v.drain()
                        v.wait_ge(S["si"].h, B["si"] + (b + 1) * NC_H)
                        v.tensor_tensor(axacc[:, 1536:2048], axacc[:, 1536:2048],
                                        hs[:, 31 * 512:32 * 512], ALU.max)
                        v.tensor_tensor(axg[:, 1536:2048], axg[:, 1536:2048],
                                        hs[:, 31 * 512:32 * 512], ALU.min)
                        v.drain()
                        v.tensor_tensor(axacc[:, 0:512], axacc[:, 0:512], axacc[:, 1024:1536], ALU.max)
                        v.tensor_tensor(axacc[:, 512:1024], axacc[:, 512:1024], axacc[:, 1536:2048], ALU.max)
                        v.tensor_tensor(axg[:, 0:512], axg[:, 0:512], axg[:, 1024:1536], ALU.min)
                        v.tensor_tensor(axg[:, 512:1024], axg[:, 512:1024], axg[:, 1536:2048], ALU.min)
                        v.drain()
                        v.tensor_tensor(axacc[:, 0:512], axacc[:, 0:512], axacc[:, 512:1024], ALU.max)
                        v.tensor_tensor(axg[:, 0:512], axg[:, 0:512], axg[:, 512:1024], ALU.min)
                        v.drain()
                        v.scalar_tensor_tensor(axf[:], axg[:, 0:512], -1.0, axacc[:, 0:512],
                                               ALU.mult, ALU.max)
                        v.drain()
                        v.nop().then_inc(S["ax"].h, 1) ; S["ax"].inc(1)
                        v.wait_ge(S["s2t"].h, B["s2t"] + 2 * b + 1)
                        for i in range(4):
                            v.reduce_max(qcols[:, AM2 + i:AM2 + i + 1],
                                         pa[:, AXT + i * 128:AXT + (i + 1) * 128], axis=AXX)
                        v.drain()
                        v.tensor_scalar(qcols[:, AM2:AM2 + 4], qcols[:, AM2:AM2 + 4], 1e-5, None, ALU.max)
                        v.drain()
                        v.reciprocal(qcols[:, S2C:S2C + 4], qcols[:, AM2:AM2 + 4])
                        v.drain()
                        v.tensor_scalar(qcols[:, S2C:S2C + 4], qcols[:, S2C:S2C + 4], 127.0, None, ALU.mult)
                        v.tensor_scalar(qcols[:, IV2:IV2 + 4], qcols[:, AM2:AM2 + 4], sc[:, 3:4], None, ALU.mult)
                        v.drain()
                        v.nop().then_inc(S["s2v"].h, 1) ; S["s2v"].inc(1)
                        v.wait_ge(S["s2t"].h, B["s2t"] + 2 * b + 2)
                        v.tensor_copy(s2row[:], pa[0:4, AXT:AXT + 128]).then_inc(S["s2v"].h, 1) ; S["s2v"].inc(1)
                        v.wait_ge(S["s2d"].h, B["s2d"] + 80 * (b + 1))
                        for chg in range(NC_H // 4):
                            if b > 0 or chg > 0:
                                v.wait_ge(S["q2"].h, B["q2"] + b * NC_H + chg * 4)
                            v.tensor_tensor(i8t[:, 0:2048], hs[:, chg * 2048:(chg + 1) * 2048],
                                            s2b[:], ALU.mult)
                            v.drain()
                            ins = v.tensor_copy(hs[:, chg * 2048:(chg + 1) * 2048], i8t[:, 0:2048])
                            ins.then_inc(S["q2"].h, 4) ; S["q2"].inc(4)
                        for tc in range(4):
                            v.wait_ge(S["m2"].h, B["m2"] + 4 * b + tc + 1)
                            tcg = b * 4 + tc
                            v.scalar_tensor_tensor(self.resid[:, tcg * D:(tcg + 1) * D],
                                                   pa[:, tc * 1024:(tc + 1) * 1024],
                                                   qcols[:, IV2 + tc:IV2 + tc + 1],
                                                   self.resid[:, tcg * D:(tcg + 1) * D],
                                                   ALU.mult, ALU.add)
                            v.drain()
                            v.nop().then_inc(S["ev2"].h, 1) ; S["ev2"].inc(1)

                @blk.scalar
                def _(s):
                    for b in range(NBLK):
                        for ch in range(NC_H):
                            idx = b * NC_H + ch
                            s.wait_ge(S["hd"].h, B["hd"] + idx + 1)
                            if b > 0 and ch == 0:
                                s.wait_ge(S["m2"].h, B["m2"] + 4 * b)
                            cth = ct[:, (ch % 2) * 512:(ch % 2) * 512 + 512]
                            s.activation(hs[:, ch * 512:(ch + 1) * 512], cth,
                                         AF.Silu).then_inc(S["si"].h, 1) ; S["si"].inc(1)





    def emit_attn_std(self, tag, W_in, NH, wo_in, attn_scale, causal=False,
                      band_in=None, relv_in=None, bo_in=None):
        P, nc = self.P, self.P.nc
        NP = NH // 2
        NKC, NQB = 16, 4
        has_band = band_in is not None
        S = {k: P.sem(f"as_{k}") for k in
             ["wle", "wlo", "wc", "pq", "pk", "qe", "ke", "ve", "vv", "sq", "un", "uv",
              "ug", "ud", "kx", "u2", "ag", "bd", "bdv", "xp", "pm", "av", "oc", "nv2", "nb", "oe",
              "og", "ogo", "oL", "mo", "oo", "bl", "rr2", "bv", "bm", "bb", "rr", "mt", "woc"]}
        B = {k: s.n for k, s in S.items()}
        xa = self.xn
        QP, KP, VA, QSQ, TMPO, PTC, OSC = 0, 2048, 4096, 6208, 8256, 10304, 12352
        Pt = xa
        ostg = xa
        opair_d = P.dram_tmp(P.uid("oprd"), [NP, 128, 2048], BF16)
        bm_d = P.dram_tmp(P.uid("bmd"), [16], F32)

        sched, kcpair_idx, av_at_end, pm_idx, bdv_idx = [], {}, {}, {}, {}
        npairs_tot, pm_n, bdv_n = 0, 0, 0
        for h in range(NH):
            for qb in range(NQB):
                kc_hi = (qb * 4 + 4) if causal else NKC
                for kc2 in range(0, kc_hi, 2):
                    npair = min(2, kc_hi - kc2)
                    tiles = []
                    for dk in range(npair):
                        kc = kc2 + dk
                        k0, q0 = kc * 128, qb * 512
                        bnd = None
                        if has_band and causal and kc >= 4 * qb - 1:
                            a = max(k0, q0)
                            b_ = min(k0 + 240, q0 + 512)
                            if b_ > a:
                                bnd = (a - q0, b_ - q0, a - k0)
                        msk = (k0 - q0) if (causal and kc >= 4 * qb) else None
                        tiles.append((kc, bnd, msk))
                    kcpair_idx[(h, qb, kc2)] = npairs_tot
                    npairs_tot += 1
                    sched.append(dict(h=h, qb=qb, kc2=kc2, npair=npair, tiles=tiles))
                    if any(tl[1] is not None for tl in tiles):
                        bdv_idx[(h, qb, kc2)] = bdv_n
                        bdv_n += 1
                    for (kc, bnd, msk) in tiles:
                        if msk is not None:
                            pm_idx[(h, qb, kc)] = pm_n
                            pm_n += 1
                av_at_end[(h, qb)] = npairs_tot

        with nc.Block(f"{tag}") as blk, \
             nc.sbuf_tensor(P.uid("ws"), [128, 2 * 1024], F32) as ws, \
             nc.sbuf_tensor(P.uid("wb"), [128, 2 * 1024], BF16) as wb, \
             nc.sbuf_tensor(P.uid("wob"), [128, NP * 1024], BF16) as wob, \
             nc.sbuf_tensor(P.uid("qa"), [65, 2 * 2048], BF16) as qa, \
             nc.sbuf_tensor(P.uid("ka"), [65, 2 * 2048], BF16) as ka, \
             nc.sbuf_tensor(P.uid("rs"), [128, 2048], F32) as rowscr, \
             nc.sbuf_tensor(P.uid("E4q"), [128, 4], BF16) as E4q, \
             nc.sbuf_tensor(P.uid("E4k"), [128, 4], BF16) as E4k, \
             nc.sbuf_tensor(P.uid("ivb"), [65, 2048], BF16) as invb, \
             nc.sbuf_tensor(P.uid("on1"), [65, 64], BF16) as ones1, \
             nc.sbuf_tensor(P.uid("ncl"), [4, 8], F32) as ncl, \
             nc.sbuf_tensor(P.uid("nmg"), [2, 8], F32) as negmarg, \
             nc.sbuf_tensor(P.uid("bnd"), [128, 2 * 240] if has_band else [1, 2], F32) as bandsb, \
             nc.sbuf_tensor(P.uid("bndf"), [128, 240] if has_band else [1, 2], F32) as bandf, \
             nc.sbuf_tensor(P.uid("bcx"), [128, 16] if has_band else [1, 2], F32) as bcolx, \
             nc.sbuf_tensor(P.uid("v31b"), [128, 16] if has_band else [1, 2], F32) as v31b, \
             nc.sbuf_tensor(P.uid("bob"), [128, 1024] if bo_in is not None else [1, 2], F32) as bob:

            s_app = self.emit_ln(blk, "atln")
            with nc.psum_tensor(P.uid("tpa"), [128, 2048], BF16) as tp_ps:
                s_t1 = self.emit_transpose(blk, self.xn, self.xT, NC_D, tp_ps,
                                           [(s_app, s_app.n)], "att1")

            @blk.vector
            def _(v):
                v.memset(E4q[:], 0.0)
                v.drain()
                v.memset(E4q[0:64, 0:1], 1.0)
                v.memset(E4q[64:128, 1:2], 1.0)
                v.drain()
                v.memset(E4k[:], 0.0)
                v.drain()
                v.memset(E4k[0:64, 2:3], 1.0)
                v.memset(E4k[64:128, 3:4], 1.0)
                v.drain()
                v.memset(ka[64:65, :], 1.0)
                v.memset(ones1[64:65, :], 1.0)
                v.wait_ge(s_t1.h, s_t1.n)
                v.memset(xa[:, VA + 64: VA + 16 * 130: 130], 1.0)
                v.memset(xa[:, VA + 129: VA + 16 * 130: 130], 1.0)
                if not has_band:
                    v.memset(negmarg[:], -1.0)
                v.nop().then_inc(S["bl"].h, 1) ; S["bl"].inc(1)

            @blk.sync
            def _(s):
                if has_band:
                    bcv = bass_rust.AP(tensor=relv_in.tensor, offset=0, ap=[[0, 128], [1, NH]])
                    s.dma_start(out=v31b[:, 0:NH], in_=bcv).then_inc(S["bv"].h, 16) ; S["bv"].inc(16)
                if bo_in is not None:
                    bcb = bass_rust.AP(tensor=bo_in.tensor, offset=0, ap=[[0, 128], [1, 1024]])
                    s.dma_start(out=bob[:], in_=bcb).then_inc(S["bv"].h, 16) ; S["bv"].inc(16)
                for lidx in range(3 * NP):
                    par = S["wle"] if lidx % 2 == 0 else S["wlo"]
                    if lidx >= 2:
                        s.wait_ge(S["wc"].h, B["wc"] + lidx - 1)
                    blkid = [lidx // 3, NP + lidx // 3, 2 * NP + lidx // 3][lidx % 3]
                    wap = bass_rust.AP(tensor=W_in.tensor, offset=blkid * 131072,
                                       ap=[[128, 128], [16384, 8], [1, 128]])
                    s.dma_start(out=ws[:, (lidx % 2) * 1024:(lidx % 2) * 1024 + 1024],
                                in_=wap).then_inc(par.h, 16) ; par.inc(16)
                for j in range(NP):
                    lidx = 3 * NP + j
                    par = S["wle"] if lidx % 2 == 0 else S["wlo"]
                    if j < 2:
                        s.wait_ge(S["wc"].h, B["wc"] + 3 * NP - 1 + j)
                    else:
                        s.wait_ge(S["woc"].h, B["woc"] + j - 1)
                    s.dma_start(out=ws[:, (lidx % 2) * 1024:(lidx % 2) * 1024 + 1024],
                                in_=wo_in[j]).then_inc(par.h, 16) ; par.inc(16)
                for tc in range(16):
                    s.wait_ge(S["oL"].h, B["oL"] + 16 * NP)
                    if tc >= 2:
                        s.wait_ge(S["mo"].h, B["mo"] + tc - 1)
                    oap = bass_rust.AP(tensor=opair_d.tensor, offset=tc * 128,
                                       ap=[[2048, 128], [128 * 2048, NP], [1, 128]])
                    par = S["og"] if tc % 2 == 0 else S["ogo"]
                    s.dma_start(out=ostg[:, OSC + (tc % 2) * 1024:OSC + (tc % 2) * 1024 + NP * 128],
                                in_=oap).then_inc(par.h, 16) ; par.inc(16)

            @blk.gpsimd
            def _(g):
                if has_band:
                    g.wait_ge(S["bv"].h, S["bv"].n)
                    for h in range(NH):
                        if h >= 1:
                            g.wait_ge(S["rr2"].h, B["rr2"] + h)
                        g.dma_start(out=bandf[:], in_=band_in[h]).then_inc(S["bm"].h, 16) ; S["bm"].inc(16)
                        g.wait_ge(S["bm"].h, S["bm"].n)
                        g.tensor_scalar(bandsb[:, 0:240], bandf[:],
                                        v31b[:, h:h + 1], 8.0, ALU.subtract, ALU.mult)
                        g.drain()
                        g.nop().then_inc(S["bb"].h, 1) ; S["bb"].inc(1)
                    g.wait_ge(S["rr"].h, B["rr"] + 1)
                    g.dma_start(out=bm_d[0:NH], in_=bcolx[0:NH, 15:16]).then_inc(S["bm"].h, 16) ; S["bm"].inc(16)
                    g.wait_ge(S["bm"].h, S["bm"].n)
                    bmap = bass_rust.AP(tensor=bm_d.tensor, offset=0, ap=[[1, 2], [2, NP]])
                    g.dma_start(out=negmarg[:, 0:NP], in_=bmap, allow_slow_non_contiguous=True).then_inc(S["bm"].h, 16) ; S["bm"].inc(16)
                    g.wait_ge(S["bm"].h, S["bm"].n)
                    g.tensor_scalar(negmarg[:, 0:NP], negmarg[:, 0:NP], 0.0, -1.0, ALU.max, ALU.mult)
                    g.drain()
                    g.tensor_scalar(negmarg[:, 0:NP], negmarg[:, 0:NP], 1.0, None, ALU.subtract)
                    g.drain()
                    g.nop().then_inc(S["bb"].h, 1) ; S["bb"].inc(1)
                for j in range(NP):
                    for wi in range(3):
                        lidx = j * 3 + wi
                        par, pb2 = (S["wle"], B["wle"]) if lidx % 2 == 0 else (S["wlo"], B["wlo"])
                        g.wait_ge(par.h, pb2 + 16 * (lidx // 2 + 1))
                        if lidx >= 2:
                            pwi = (lidx - 2) % 3
                            pj = (lidx - 2) // 3
                            tgt = [B["pq"] + pj * 4 + 4, B["pk"] + pj * 4 + 4,
                                   B["ve"] + pj * 16 + 16][pwi]
                            g.wait_ge([S["pq"], S["pk"], S["ve"]][pwi].h, tgt)
                        g.tensor_copy(wb[:, (lidx % 2) * 1024:(lidx % 2) * 1024 + 1024],
                                      ws[:, (lidx % 2) * 1024:(lidx % 2) * 1024 + 1024])
                        g.drain()
                        g.nop().then_inc(S["wc"].h, 1) ; S["wc"].inc(1)
                    for hh in range(2):
                        h = 2 * j + hh
                        rot = h % 2
                        if h >= 2:
                            g.wait_ge(S["av"].h, B["av"] + av_at_end[(h - 2, NQB - 1)])
                        g.wait_ge(S["qe"].h, B["qe"] + j * 4 + 4)
                        g.dma_start(out=qa[0:64, rot * 2048:(rot + 1) * 2048],
                                    in_=xa[hh * 64:(hh + 1) * 64, QP:QP + 2048]
                                    ).then_inc(S["ag"].h, 16) ; S["ag"].inc(16)
                        g.wait_ge(S["ke"].h, B["ke"] + j * 4 + 4)
                        g.dma_start(out=ka[0:64, rot * 2048:(rot + 1) * 2048],
                                    in_=xa[hh * 64:(hh + 1) * 64, KP:KP + 2048]
                                    ).then_inc(S["ag"].h, 16) ; S["ag"].inc(16)
                        if hh == 0:
                            g.wait_ge(S["ud"].h, B["ud"] + j + 1)
                            g.dma_start(out=ncl[0:2, 2:3], in_=ncl[2:4, 0:1],
                                        allow_slow_non_contiguous=True).then_inc(S["kx"].h, 16) ; S["kx"].inc(16)
                            g.wait_ge(S["kx"].h, S["kx"].n)
                        g.wait_ge(S["u2"].h, B["u2"] + j + 1)
                        g.dma_start(out=qa[64:65, rot * 2048:(rot + 1) * 2048],
                                    in_=rowscr[hh:hh + 1, 0:2048]).then_inc(S["ag"].h, 16) ; S["ag"].inc(16)
                        if has_band:
                            if h >= 2:
                                g.wait_ge(S["av"].h, B["av"] + av_at_end[(h - 2, NQB - 1)])
                            g.dma_start(out=bandf[:], in_=band_in[h]).then_inc(S["bm"].h, 16) ; S["bm"].inc(16)
                            g.wait_ge(S["bm"].h, S["bm"].n)
                            g.tensor_scalar(bandsb[:, rot * 240:rot * 240 + 240], bandf[:],
                                            v31b[:, h:h + 1], 8.0, ALU.subtract, ALU.mult)
                            g.drain()
                            g.nop().then_inc(S["bb"].h, 1) ; S["bb"].inc(1)
                        g.wait_ge(S["oe"].h, B["oe"] + h * NQB + NQB)
                        g.dma_start(out=opair_d[j, hh * 64:(hh + 1) * 64, :],
                                    in_=xa[0:64, TMPO:TMPO + 2048]).then_inc(S["oL"].h, 16) ; S["oL"].inc(16)
                        g.wait_ge(S["oL"].h, S["oL"].n)
                for j in range(NP):
                    lidx = 3 * NP + j
                    par, pb2 = (S["wle"], B["wle"]) if lidx % 2 == 0 else (S["wlo"], B["wlo"])
                    g.wait_ge(par.h, pb2 + 16 * (lidx // 2 + 1))
                    g.tensor_copy(wob[:, j * 1024:(j + 1) * 1024],
                                  ws[:, (lidx % 2) * 1024:(lidx % 2) * 1024 + 1024])
                    g.drain()
                    g.nop().then_inc(S["woc"].h, 1) ; S["woc"].inc(1)

            with nc.psum_tensor(P.uid("pa2"), [128, 4096], F32) as pa2:
                @blk.tensor
                def _(t):
                    t.wait_ge(s_t1.h, s_t1.n)
                    t.wait_ge(S["bl"].h, S["bl"].n)
                    if has_band:
                        t.wait_ge(S["rr2"].h, B["rr2"] + NH)
                        t.transpose(pa2[0:NH, 2048:2176], bcolx[:, 0:NH],
                                    self.eye_f32[:]).then_inc(S["mt"].h, 1) ; S["mt"].inc(1)
                    for j in range(NP):
                        for wi in range(2):
                            lidx = j * 3 + wi
                            esem, ename = (S["qe"], "qe") if wi == 0 else (S["ke"], "ke")
                            psem = S["pq"] if wi == 0 else S["pk"]
                            t.wait_ge(S["wc"].h, B["wc"] + lidx + 1)
                            for q4 in range(4):
                                pp = pa2[:, (q4 % 2) * 512:(q4 % 2) * 512 + 512]
                                if q4 >= 2:
                                    t.wait_ge(esem.h, B[ename] + j * 4 + q4 - 1)
                                elif wi == 0 and j >= 1:
                                    t.wait_ge(S["vv"].h, B["vv"] + (j - 1) * 16 + 15 + q4)
                                    if q4 == 0:
                                        t.wait_ge(S["oe"].h, B["oe"] + 8 * j)
                                elif wi == 1:
                                    t.wait_ge(S["qe"].h, B["qe"] + j * 4 + 3 + q4)
                                for kf in range(NC_D):
                                    ins = t.matmul(pp, wb[:, (lidx % 2) * 1024 + kf * 128:(lidx % 2) * 1024 + kf * 128 + 128],
                                                   self.xT[:, kf * NTOK + q4 * 512: kf * NTOK + (q4 + 1) * 512],
                                                   start=(kf == 0), stop=(kf == NC_D - 1))
                                ins.then_inc(psem.h, 1) ; psem.inc(1)
                        lidx = j * 3 + 2
                        t.wait_ge(S["wc"].h, B["wc"] + lidx + 1)
                        for tc in range(16):
                            pv = pa2[:, (tc % 2) * 512:(tc % 2) * 512 + 128]
                            if tc >= 2:
                                t.wait_ge(S["vv"].h, B["vv"] + j * 16 + tc - 1)
                            else:
                                t.wait_ge(S["ke"].h, B["ke"] + j * 4 + 4)
                            for kf in range(NC_D):
                                ins = t.matmul(pv, self.xT[:, kf * NTOK + tc * 128: kf * NTOK + (tc + 1) * 128],
                                               wb[:, (lidx % 2) * 1024 + kf * 128:(lidx % 2) * 1024 + kf * 128 + 128],
                                               start=(kf == 0), stop=(kf == NC_D - 1))
                            ins.then_inc(S["ve"].h, 1) ; S["ve"].inc(1)
                        t.wait_ge(S["sq"].h, B["sq"] + j * 2 + 2)
                        for q4 in range(4):
                            if j >= 1 and q4 == 0:
                                t.wait_ge(S["oe"].h, B["oe"] + (2 * j - 1) * NQB + NQB)
                            if j * 4 + q4 >= 1:
                                t.wait_ge(S["uv"].h, B["uv"] + j * 4 + q4)
                            t.matmul(pa2[0:4, 1536:2048], E4q[:],
                                     xa[:, QSQ + q4 * 512:QSQ + (q4 + 1) * 512],
                                     start=True, stop=False)
                            ins = t.matmul(pa2[0:4, 1536:2048], E4k[:],
                                           xa[:, TMPO + q4 * 512:TMPO + (q4 + 1) * 512],
                                           start=False, stop=True)
                            ins.then_inc(S["un"].h, 1) ; S["un"].inc(1)
                        for hh in range(2):
                            h = 2 * j + hh
                            rot = h % 2
                            t.wait_ge(S["ag"].h, B["ag"] + (h + 1) * 48)
                            if has_band:
                                t.wait_ge(S["bb"].h, B["bb"] + NH + 1 + h + 1)
                            for qb in range(NQB):
                                kc_hi = (qb * 4 + 4) if causal else NKC
                                slot = (h * NQB + qb) % 2
                                po = pa2[0:65, 1024 + slot * 512: 1024 + slot * 512 + 512]
                                if h * NQB + qb >= 2:
                                    t.wait_ge(S["oc"].h, B["oc"] + h * NQB + qb - 1)
                                for kc2 in range(0, kc_hi, 2):
                                    srot = (kc2 // 2) % 2
                                    idx = kcpair_idx[(h, qb, kc2)]
                                    npair = min(2, kc_hi - kc2)
                                    if idx >= 2:
                                        t.wait_ge(S["xp"].h, B["xp"] + idx - 1)
                                    for dk in range(npair):
                                        kc = kc2 + dk
                                        ins = t.matmul(pa2[:, 2048 + srot * 1024 + dk * 512: 2048 + srot * 1024 + (dk + 1) * 512],
                                                       ka[:, rot * 2048 + kc * 128: rot * 2048 + kc * 128 + 128],
                                                       qa[:, rot * 2048 + qb * 512: rot * 2048 + (qb + 1) * 512],
                                                       start=True, stop=True)
                                    ins.then_inc(S["bd"].h, 1) ; S["bd"].inc(1)
                                    for dk in range(npair):
                                        kc = kc2 + dk
                                        _, bnd, msk = sched[idx]["tiles"][dk]
                                        t.wait_ge(S["xp"].h, B["xp"] + idx + 1)
                                        if msk is not None:
                                            t.wait_ge(S["pm"].h, B["pm"] + pm_idx[(h, qb, kc)] + 1)
                                        ins = t.matmul(po,
                                                       xa[:, VA + kc * 130 + hh * 65: VA + kc * 130 + hh * 65 + 65],
                                                       Pt[:, PTC + srot * 1024 + dk * 512: PTC + srot * 1024 + (dk + 1) * 512],
                                                       start=(kc == 0), stop=(kc == kc_hi - 1))
                                        if dk == npair - 1:
                                            ins.then_inc(S["av"].h, 1) ; S["av"].inc(1)
                            t.wait_ge(S["nv2"].h, B["nv2"] + h + 1)
                            for qb in range(NQB):
                                bc_i = h * NQB + qb
                                if bc_i >= 2:
                                    t.wait_ge(S["oe"].h, B["oe"] + bc_i - 1)
                                ins = t.matmul(pa2[0:64, (qb % 2) * 512: (qb % 2) * 512 + 512],
                                               ones1[64:65, :], invb[64:65, qb * 512:(qb + 1) * 512],
                                               start=True, stop=True)
                                ins.then_inc(S["nb"].h, 1) ; S["nb"].inc(1)
                    t.wait_ge(S["woc"].h, B["woc"] + NP)
                    for tc in range(16):
                        par, pb3 = (S["og"], B["og"]) if tc % 2 == 0 else (S["ogo"], B["ogo"])
                        t.wait_ge(par.h, pb3 + 16 * (tc // 2 + 1))
                        pout = pa2[:, 2048 + (tc % 2) * 1024:2048 + (tc % 2) * 1024 + 1024]
                        if tc >= 2:
                            t.wait_ge(S["oo"].h, B["oo"] + tc - 1)
                        for j2 in range(NP):
                            for oh in range(2):
                                ins = t.matmul(pout[:, oh * 512:(oh + 1) * 512],
                                               ostg[:, OSC + (tc % 2) * 1024 + j2 * 128:OSC + (tc % 2) * 1024 + (j2 + 1) * 128],
                                               wob[:, j2 * 1024 + oh * 512:j2 * 1024 + (oh + 1) * 512],
                                               start=(j2 == 0), stop=(j2 == NP - 1))
                        ins.then_inc(S["mo"].h, 1) ; S["mo"].inc(1)

                @blk.vector
                def _(v):
                    if has_band:
                        for h in range(NH):
                            v.wait_ge(S["bb"].h, B["bb"] + h + 1)
                            v.reduce_max(bcolx[:, h:h + 1], bandsb[:, 0:240], axis=AXX)
                            v.drain()
                            v.nop().then_inc(S["rr2"].h, 1) ; S["rr2"].inc(1)
                        v.wait_ge(S["mt"].h, B["mt"] + 1)
                        v.reduce_max(bcolx[0:NH, 15:16], pa2[0:NH, 2048:2176], axis=AXX)
                        v.drain()
                        v.nop().then_inc(S["rr"].h, 1) ; S["rr"].inc(1)
                    for j in range(NP):
                        for tc in range(16):
                            v.wait_ge(S["ve"].h, B["ve"] + j * 16 + tc + 1)
                            pv = pa2[:, (tc % 2) * 512:(tc % 2) * 512 + 128]
                            v.tensor_copy(xa[:, VA + tc * 130: VA + tc * 130 + 64], pv[:, 0:64])
                            ins = v.tensor_copy(xa[:, VA + tc * 130 + 65: VA + tc * 130 + 129], pv[:, 64:128])
                            ins.then_inc(S["vv"].h, 1) ; S["vv"].inc(1)
                        for q4 in range(4):
                            v.wait_ge(S["un"].h, B["un"] + j * 4 + q4 + 1)
                            ins = v.tensor_copy(rowscr[0:4, q4 * 512:(q4 + 1) * 512], pa2[0:4, 1536:2048])
                            ins.then_inc(S["uv"].h, 1) ; S["uv"].inc(1)
                        v.wait_ge(S["ug"].h, B["ug"] + j + 1)
                        v.reduce_max(ncl[0:4, 0:1], rowscr[0:4, 0:2048], axis=AXX)
                        v.drain()
                        v.nop().then_inc(S["ud"].h, 1) ; S["ud"].inc(1)
                        v.wait_ge(S["kx"].h, B["kx"] + 16 * (j + 1))
                        v.tensor_scalar(ncl[0:2, 1:2], ncl[0:2, 2:3], -1.0, None, ALU.mult)
                        v.drain()
                        if has_band:
                            v.wait_ge(S["bb"].h, B["bb"] + NH + 1)
                        v.tensor_scalar(rowscr[0:2, 0:2048], rowscr[0:2, 0:2048], ncl[0:2, 1:2],
                                        negmarg[0:2, j:j + 1], ALU.mult, ALU.add)
                        v.drain()
                        v.nop().then_inc(S["u2"].h, 1) ; S["u2"].inc(1)
                        for hh in range(2):
                            h = 2 * j + hh
                            rot = h % 2
                            for qb in range(NQB):
                                kc_hi = (qb * 4 + 4) if causal else NKC
                                for kc2 in range(0, kc_hi, 2):
                                    idx = kcpair_idx[(h, qb, kc2)]
                                    srot = (kc2 // 2) % 2
                                    npair = min(2, kc_hi - kc2)
                                    if any(tl[1] is not None for tl in sched[idx]["tiles"]):
                                        v.wait_ge(S["bd"].h, B["bd"] + idx + 1)
                                        for dk in range(npair):
                                            kc, bnd, msk = sched[idx]["tiles"][dk]
                                            if bnd is None:
                                                continue
                                            a, b_, bs = bnd
                                            base = 2048 + srot * 1024 + dk * 512
                                            v.tensor_tensor(pa2[:, base + a: base + b_],
                                                            pa2[:, base + a: base + b_],
                                                            bandsb[:, rot * 240 + bs: rot * 240 + bs + (b_ - a)],
                                                            ALU.add)
                                        v.drain()
                                        v.nop().then_inc(S["bdv"].h, 1) ; S["bdv"].inc(1)
                                    for dk in range(npair):
                                        kc, bnd, msk = sched[idx]["tiles"][dk]
                                        if msk is None:
                                            continue
                                        v.wait_ge(S["xp"].h, B["xp"] + idx + 1)
                                        base = PTC + srot * 1024 + dk * 512
                                        if msk > 0:
                                            v.memset(Pt[:, base: base + msk], 0.0)
                                        ins = v.tensor_tensor(Pt[:, base + msk: base + msk + 128],
                                                              Pt[:, base + msk: base + msk + 128],
                                                              self.tri_bf[:], ALU.mult)
                                        ins.then_inc(S["pm"].h, 1) ; S["pm"].inc(1)
                                v.wait_ge(S["av"].h, B["av"] + av_at_end[(h, qb)])
                                if h >= 1:
                                    g_prev = 16 * h
                                    v.wait_ge(S["oL"].h, B["oL"] + g_prev)
                                slot = (h * NQB + qb) % 2
                                po = pa2[0:65, 1024 + slot * 512: 1024 + slot * 512 + 512]
                                ins = v.tensor_copy(xa[0:65, TMPO + qb * 512:TMPO + (qb + 1) * 512], po[:])
                                ins.then_inc(S["oc"].h, 1) ; S["oc"].inc(1)
                                v.wait_ge(S["oc"].h, B["oc"] + h * NQB + qb + 1)
                                if h >= 1:
                                    v.wait_ge(S["nb"].h, B["nb"] + h * NQB)
                                with self.P.nc.allow_low_precision(reason="bf16 invden"):
                                    rins = v.reciprocal(invb[64:65, qb * 512:(qb + 1) * 512],
                                                        xa[64:65, TMPO + qb * 512:TMPO + (qb + 1) * 512])
                                if qb == NQB - 1:
                                    rins.then_inc(S["nv2"].h, 1) ; S["nv2"].inc(1)
                            for qb in range(NQB):
                                v.wait_ge(S["nb"].h, B["nb"] + h * NQB + qb + 1)
                                sl = xa[0:64, TMPO + qb * 512:TMPO + (qb + 1) * 512]
                                ins = v.tensor_tensor(sl, sl, pa2[0:64, (qb % 2) * 512: (qb % 2) * 512 + 512],
                                                      ALU.mult)
                                ins.then_inc(S["oe"].h, 1) ; S["oe"].inc(1)
                    for tc in range(16):
                        v.wait_ge(S["mo"].h, B["mo"] + tc + 1)
                        rsl = self.resid[:, tc * D:(tc + 1) * D]
                        v.tensor_tensor(rsl, rsl, pa2[:, 2048 + (tc % 2) * 1024:2048 + (tc % 2) * 1024 + 1024],
                                        ALU.add)
                        v.drain()
                        if bo_in is not None:
                            v.wait_ge(S["bv"].h, S["bv"].n)
                            ins = v.tensor_tensor(rsl, rsl, bob[:], ALU.add)
                        ins.then_inc(S["oo"].h, 1) ; S["oo"].inc(1)

                @blk.scalar
                def _(s):
                    for j in range(NP):
                        for wi in range(2):
                            esem, psem, pname = ((S["qe"], S["pq"], "pq") if wi == 0
                                                 else (S["ke"], S["pk"], "pk"))
                            if wi == 0 and j >= 1:
                                s.wait_ge(S["ag"].h, B["ag"] + 2 * j * 48)
                            for q4 in range(4):
                                s.wait_ge(psem.h, B[pname] + j * 4 + q4 + 1)
                                dst = xa[:, (QP if wi == 0 else KP) + q4 * 512:
                                         (QP if wi == 0 else KP) + (q4 + 1) * 512]
                                if wi == 0 and attn_scale != 1.0:
                                    ins = s.activation(dst, pa2[:, (q4 % 2) * 512:(q4 % 2) * 512 + 512],
                                                       AF.Identity, scale=float(attn_scale))
                                else:
                                    ins = s.activation(dst, pa2[:, (q4 % 2) * 512:(q4 % 2) * 512 + 512],
                                                       AF.Identity)
                                ins.then_inc(esem.h, 1) ; esem.inc(1)
                            if wi == 1 and j > 0:
                                s.wait_ge(S["oL"].h, B["oL"] + 32 * j)
                            if wi == 0 and j > 0:
                                s.wait_ge(S["un"].h, B["un"] + j * 4)
                            s.wait_ge(esem.h, B[pname.replace("p", "") if False else ("qe" if wi == 0 else "ke")] + j * 4 + 4)
                            ins = s.activation(xa[:, (QSQ if wi == 0 else TMPO):(QSQ if wi == 0 else TMPO) + 2048],
                                               xa[:, (QP if wi == 0 else KP):(QP if wi == 0 else KP) + 2048],
                                               AF.Square)
                            ins.then_inc(S["sq"].h, 1) ; S["sq"].inc(1)
                        s.wait_ge(S["uv"].h, B["uv"] + j * 4 + 4)
                        ins = s.activation(rowscr[0:4, 0:2048], rowscr[0:4, 0:2048], AF.Sqrt)
                        ins.then_inc(S["ug"].h, 1) ; S["ug"].inc(1)
                        for hh in range(2):
                            h = 2 * j + hh
                            for qb in range(NQB):
                                kc_hi = (qb * 4 + 4) if causal else NKC
                                for kc2 in range(0, kc_hi, 2):
                                    idx = kcpair_idx[(h, qb, kc2)]
                                    srot = (kc2 // 2) % 2
                                    npair = min(2, kc_hi - kc2)
                                    if any(tl[1] is not None for tl in sched[idx]["tiles"]):
                                        s.wait_ge(S["bdv"].h, B["bdv"] + bdv_idx[(h, qb, kc2)] + 1)
                                    else:
                                        s.wait_ge(S["bd"].h, B["bd"] + idx + 1)
                                    if idx >= 2:
                                        s.wait_ge(S["av"].h, B["av"] + idx - 1)
                                    ins = s.activation(Pt[:, PTC + srot * 1024: PTC + srot * 1024 + npair * 512],
                                                       pa2[:, 2048 + srot * 1024: 2048 + srot * 1024 + npair * 512],
                                                       AF.Exp)
                                    ins.then_inc(S["xp"].h, 1) ; S["xp"].inc(1)


    def emit_attn_l0b(self, tag, W_in, wo_in, qkb_in, bv_in, bo_in):
        P, nc = self.P, self.P.nc
        NH, NP, NQB = 16, 8, 4
        S = {k: P.sem(f"as_{k}") for k in
             ["wle", "wlo", "wc", "woc", "pq", "pk", "qe", "ke", "ve", "vv", "sq", "un",
              "uv", "ug", "ud", "kx", "u2", "ag", "bd", "xp", "av", "oc", "nv2", "nb",
              "oe", "og", "ogo", "oL", "mo", "oo", "bl", "bv"]}
        for k in ["bdj", "xpi", "pmi", "avi", "ocB", "kmv", "km2", "vt", "vte", "vb"]:
            S[k] = P.sem(f"ab_{k}")
        B = {k: s.n for k, s in S.items()}
        xa = self.xn
        QP, KP, VA, QSQ = 0, 2048, 4096, 6208
        TMPA, TMPB, PTI, OSC, PTJ = 8256, 10304, 12352, 13376, 15424
        opair_d = P.dram_tmp(P.uid("oprd"), [NP, 128, 2048], BF16)
        cmask = (np.arange(16)[:, None] <= (np.arange(2048)[None, :] // 128)).astype(np.float32)
        cm_in = P.const_in("c_cmask", cmask)

        if DBG_TAP:
            dbg_sc = P.dram_out("dbg_sc", [128, 512])
            dbg_pt = P.dram_out("dbg_pt", [128, 512], BF16)
            dbg_dn = P.dram_out("dbg_dn", [1, 2048], BF16)
            S["tap"] = P.sem("ab_tap")
        with ExitStack() as es:
            def sb(n, sh, dt):
                return es.enter_context(nc.sbuf_tensor(P.uid(n), sh, dt))
            ws = sb("ws", [128, 2 * 1024], F32)
            wb = sb("wb", [128, 2 * 1024], BF16)
            wob = sb("wob", [128, NP * 1024], BF16)
            qa = sb("qa", [65, 2 * 2048], BF16)
            ka = sb("ka", [65, 2 * 2048], BF16)
            rowscr = sb("rs", [128, 2048], F32)
            E4q = sb("E4q", [128, 4], BF16)
            E4k = sb("E4k", [128, 4], BF16)
            invb = sb("ivb", [65, 2 * 2048], BF16)
            ones1 = sb("on1", [65, 64], BF16)
            ncl = sb("ncl", [4, 8], F32)
            negmarg = sb("nmg", [2, 8], F32)
            kmf = sb("kmf", [128, 16], F32)
            kmb = sb("kmb", [128, 16], BF16)
            kma = sb("kma", [65, 32], BF16)
            vma = sb("vma", [16, 130], BF16)
            cmaskb = sb("cmk", [16, 2048], BF16)
            qkb = sb("qkb", [128, 16], F32)
            vbt = sb("vbt", [128, 128], F32)
            eslab = sb("esl", [128, 31], BF16)
            bob = sb("bob", [128, 1024], F32)
            blk = es.enter_context(nc.Block(f"{tag}"))

            s_app = self.emit_ln(blk, "atln")
            with nc.psum_tensor(P.uid("tpb"), [128, 2048], BF16) as tp_ps:
                s_t1 = self.emit_transpose(blk, self.xn, self.xT, NC_D, tp_ps,
                                           [(s_app, s_app.n)], "att1")

            @blk.vector
            def _(v):
                v.memset(E4q[:], 0.0)
                v.drain()
                v.memset(E4q[0:64, 0:1], 1.0)
                v.memset(E4q[64:128, 1:2], 1.0)
                v.drain()
                v.memset(E4k[:], 0.0)
                v.drain()
                v.memset(E4k[0:64, 2:3], 1.0)
                v.memset(E4k[64:128, 3:4], 1.0)
                v.drain()
                v.memset(ka[64:65, :], 1.0)
                v.memset(ones1[64:65, :], 1.0)
                v.memset(kma[64:65, :], 1.0)
                v.memset(negmarg[:], -1.0)
                v.memset(eslab[:], 0.0)
                v.drain()
                v.memset(eslab[:, 15:16], 0.0078125)
                v.drain()
                v.wait_ge(S["bv"].h, B["bv"] + 32)
                v.tensor_scalar(qkb[:, 0:8], qkb[:, 0:8], 0.125, None, ALU.mult)
                v.drain()
                v.wait_ge(s_t1.h, s_t1.n)
                v.memset(xa[:, VA + 64: VA + 16 * 130: 130], 1.0)
                v.memset(xa[:, VA + 129: VA + 16 * 130: 130], 1.0)
                v.nop().then_inc(S["bl"].h, 1) ; S["bl"].inc(1)

            @blk.sync
            def _(s):
                s.dma_start(out=qkb[:], in_=qkb_in[:]).then_inc(S["bv"].h, 16) ; S["bv"].inc(16)
                bcb = bass_rust.AP(tensor=bo_in.tensor, offset=0, ap=[[0, 128], [1, 1024]])
                s.dma_start(out=bob[:], in_=bcb).then_inc(S["bv"].h, 16) ; S["bv"].inc(16)
                for lidx in range(3 * NP):
                    par = S["wle"] if lidx % 2 == 0 else S["wlo"]
                    if lidx >= 2:
                        s.wait_ge(S["wc"].h, B["wc"] + lidx - 1)
                    blkid = [lidx // 3, NP + lidx // 3, 2 * NP + lidx // 3][lidx % 3]
                    wap = bass_rust.AP(tensor=W_in.tensor, offset=blkid * 131072,
                                       ap=[[128, 128], [16384, 8], [1, 128]])
                    s.dma_start(out=ws[:, (lidx % 2) * 1024:(lidx % 2) * 1024 + 1024],
                                in_=wap).then_inc(par.h, 16) ; par.inc(16)
                for j in range(NP):
                    lidx = 3 * NP + j
                    par = S["wle"] if lidx % 2 == 0 else S["wlo"]
                    if j < 2:
                        s.wait_ge(S["wc"].h, B["wc"] + 3 * NP - 1 + j)
                    else:
                        s.wait_ge(S["woc"].h, B["woc"] + j - 1)
                    s.dma_start(out=ws[:, (lidx % 2) * 1024:(lidx % 2) * 1024 + 1024],
                                in_=wo_in[j]).then_inc(par.h, 16) ; par.inc(16)
                for tc in range(16):
                    s.wait_ge(S["oL"].h, B["oL"] + 32 * NP)
                    if tc >= 2:
                        s.wait_ge(S["mo"].h, B["mo"] + tc - 1)
                    oap = bass_rust.AP(tensor=opair_d.tensor, offset=tc * 128,
                                       ap=[[2048, 128], [128 * 2048, NP], [1, 128]])
                    par = S["og"] if tc % 2 == 0 else S["ogo"]
                    s.dma_start(out=xa[:, OSC + (tc % 2) * 1024:OSC + (tc % 2) * 1024 + NP * 128],
                                in_=oap).then_inc(par.h, 16) ; par.inc(16)

            @blk.gpsimd
            def _(g):
                g.dma_start(out=cmaskb[:], in_=cm_in[:]).then_inc(S["vb"].h, 16) ; S["vb"].inc(16)
                for j in range(NP):
                    if j >= 1:
                        g.wait_ge(S["vv"].h, B["vv"] + 16 * j)
                    vbap = bass_rust.AP(tensor=bv_in.tensor, offset=j * 128,
                                        ap=[[0, 128], [1, 128]])
                    g.dma_start(out=vbt[:], in_=vbap).then_inc(S["vb"].h, 16) ; S["vb"].inc(16)
                    for wi in range(3):
                        lidx = j * 3 + wi
                        par, pb2 = (S["wle"], B["wle"]) if lidx % 2 == 0 else (S["wlo"], B["wlo"])
                        g.wait_ge(par.h, pb2 + 16 * (lidx // 2 + 1))
                        if lidx >= 2:
                            pwi = (lidx - 2) % 3
                            pj = (lidx - 2) // 3
                            tgt = [B["pq"] + pj * 4 + 4, B["pk"] + pj * 4 + 4,
                                   B["ve"] + pj * 16 + 16][pwi]
                            g.wait_ge([S["pq"], S["pk"], S["ve"]][pwi].h, tgt)
                        g.tensor_copy(wb[:, (lidx % 2) * 1024:(lidx % 2) * 1024 + 1024],
                                      ws[:, (lidx % 2) * 1024:(lidx % 2) * 1024 + 1024])
                        g.drain()
                        g.nop().then_inc(S["wc"].h, 1) ; S["wc"].inc(1)
                    for hh in range(2):
                        h = 2 * j + hh
                        rot = h % 2
                        if h >= 2:
                            g.wait_ge(S["av"].h, B["av"] + (h - 1) * NQB)
                            if not DBG_INTRA_ONLY:
                                g.wait_ge(S["avi"].h, B["avi"] + (h - 1) * NQB)
                        g.wait_ge(S["qe"].h, B["qe"] + j * 4 + 4)
                        g.dma_start(out=qa[0:64, rot * 2048:(rot + 1) * 2048],
                                    in_=xa[hh * 64:(hh + 1) * 64, QP:QP + 2048]
                                    ).then_inc(S["ag"].h, 16) ; S["ag"].inc(16)
                        g.wait_ge(S["ke"].h, B["ke"] + j * 4 + 4)
                        g.dma_start(out=ka[0:64, rot * 2048:(rot + 1) * 2048],
                                    in_=xa[hh * 64:(hh + 1) * 64, KP:KP + 2048]
                                    ).then_inc(S["ag"].h, 16) ; S["ag"].inc(16)
                        if hh == 0:
                            g.wait_ge(S["ud"].h, B["ud"] + j + 1)
                            g.dma_start(out=ncl[0:2, 2:3], in_=ncl[2:4, 0:1],
                                        allow_slow_non_contiguous=True).then_inc(S["kx"].h, 16) ; S["kx"].inc(16)
                            g.wait_ge(S["kx"].h, S["kx"].n)
                        g.wait_ge(S["u2"].h, B["u2"] + j + 1)
                        g.dma_start(out=qa[64:65, rot * 2048:(rot + 1) * 2048],
                                    in_=rowscr[hh:hh + 1, 0:2048]).then_inc(S["ag"].h, 16) ; S["ag"].inc(16)
                        if DBG_TAP and h == 15:
                            g.wait_ge(S["xp"].h, B["xp"] + 15 * 4 + 2)
                            g.dma_start(out=dbg_pt[:], in_=xa[:, 12352 + 512:12352 + 1024]
                                        ).then_inc(S["tap"].h, 16) ; S["tap"].inc(16)
                            g.wait_ge(S["oc"].h, B["oc"] + 15 * 4 + 4)
                            g.dma_start(out=dbg_dn[:], in_=xa[64:65, 8256:8256 + 2048]
                                        ).then_inc(S["tap"].h, 16) ; S["tap"].inc(16)
                        if not DBG_INTRA_ONLY:
                            g.wait_ge(S["kmv"].h, B["kmv"] + j + 1)
                            if h >= 2:
                                g.wait_ge(S["bdj"].h, B["bdj"] + (h - 1) * NQB)
                            g.dma_start(out=kma[0:64, rot * 16:(rot + 1) * 16],
                                        in_=kmb[hh * 64:(hh + 1) * 64, :]).then_inc(S["km2"].h, 16)
                            S["km2"].inc(16)
                        g.wait_ge(S["oe"].h, B["oe"] + h * NQB + NQB)
                        g.dma_start(out=opair_d[j, hh * 64:(hh + 1) * 64, :],
                                    in_=xa[0:64, TMPA:TMPA + 2048]).then_inc(S["oL"].h, 16) ; S["oL"].inc(16)
                        g.wait_ge(S["oL"].h, S["oL"].n)
                for j in range(NP):
                    lidx = 3 * NP + j
                    par, pb2 = (S["wle"], B["wle"]) if lidx % 2 == 0 else (S["wlo"], B["wlo"])
                    g.wait_ge(par.h, pb2 + 16 * (lidx // 2 + 1))
                    g.tensor_copy(wob[:, j * 1024:(j + 1) * 1024],
                                  ws[:, (lidx % 2) * 1024:(lidx % 2) * 1024 + 1024])
                    g.drain()
                    g.nop().then_inc(S["woc"].h, 1) ; S["woc"].inc(1)

            pa2 = es.enter_context(nc.psum_tensor(P.uid("pb2"), [128, 4096], F32))
            @blk.tensor
            def _(t):
                t.wait_ge(s_t1.h, s_t1.n)
                t.wait_ge(S["bl"].h, S["bl"].n)
                for j in range(NP):
                    for wi in range(2):
                        lidx = j * 3 + wi
                        esem, ename = (S["qe"], "qe") if wi == 0 else (S["ke"], "ke")
                        psem = S["pq"] if wi == 0 else S["pk"]
                        t.wait_ge(S["wc"].h, B["wc"] + lidx + 1)
                        for q4 in range(4):
                            pp = pa2[:, (q4 % 2) * 512:(q4 % 2) * 512 + 512]
                            if q4 >= 2:
                                t.wait_ge(esem.h, B[ename] + j * 4 + q4 - 1)
                            elif wi == 0 and j >= 1:
                                t.wait_ge(S["vv"].h, B["vv"] + (j - 1) * 16 + 15 + q4)
                                if DBG_VMT:
                                    t.wait_ge(S["vte"].h, B["vte"] + 2 * j)
                            elif wi == 1:
                                t.wait_ge(S["qe"].h, B["qe"] + j * 4 + 3 + q4)
                            for kf in range(NC_D):
                                ins = t.matmul(pp, wb[:, (lidx % 2) * 1024 + kf * 128:(lidx % 2) * 1024 + kf * 128 + 128],
                                               self.xT[:, kf * NTOK + q4 * 512: kf * NTOK + (q4 + 1) * 512],
                                               start=(kf == 0), stop=(kf == NC_D - 1))
                            ins.then_inc(psem.h, 1) ; psem.inc(1)
                    lidx = j * 3 + 2
                    t.wait_ge(S["wc"].h, B["wc"] + lidx + 1)
                    for tc in range(16):
                        pv = pa2[:, (tc % 2) * 512:(tc % 2) * 512 + 128]
                        if tc >= 2:
                            t.wait_ge(S["vv"].h, B["vv"] + j * 16 + tc - 1)
                        else:
                            t.wait_ge(S["ke"].h, B["ke"] + j * 4 + 4)
                        for kf in range(NC_D):
                            ins = t.matmul(pv, self.xT[:, kf * NTOK + tc * 128: kf * NTOK + (tc + 1) * 128],
                                           wb[:, (lidx % 2) * 1024 + kf * 128:(lidx % 2) * 1024 + kf * 128 + 128],
                                           start=(kf == 0), stop=(kf == NC_D - 1))
                        ins.then_inc(S["ve"].h, 1) ; S["ve"].inc(1)
                    t.wait_ge(S["sq"].h, B["sq"] + j * 2 + 2)
                    if j >= 1:
                        t.wait_ge(S["xp"].h, B["xp"] + 8 * j)
                    for q4 in range(4):
                        if j * 4 + q4 >= 1:
                            t.wait_ge(S["uv"].h, B["uv"] + j * 4 + q4)
                        t.matmul(pa2[0:4, 1536:2048], E4q[:],
                                 xa[:, QSQ + q4 * 512:QSQ + (q4 + 1) * 512],
                                 start=True, stop=False)
                        ins = t.matmul(pa2[0:4, 1536:2048], E4k[:],
                                       xa[:, TMPA + q4 * 512:TMPA + (q4 + 1) * 512],
                                       start=False, stop=True)
                        ins.then_inc(S["un"].h, 1) ; S["un"].inc(1)
                    if DBG_VMT:
                        t.wait_ge(S["vv"].h, B["vv"] + j * 16 + 16)
                        for hh in range(2):
                            if j >= 1:
                                t.wait_ge(S["vte"].h, B["vte"] + 2 * (j - 1) + hh + 1)
                            for c in range(16):
                                ins = t.matmul(pa2[0:16, hh * 512:hh * 512 + 65],
                                               eslab[:, 15 - c:31 - c],
                                               xa[:, VA + c * 130 + hh * 65: VA + c * 130 + hh * 65 + 65],
                                               start=(c == 0), stop=(c == 15))
                            ins.then_inc(S["vt"].h, 1) ; S["vt"].inc(1)
                    for hh in range(2):
                        h = 2 * j + hh
                        rot = h % 2
                        t.wait_ge(S["ag"].h, B["ag"] + (h + 1) * 48)
                        if not DBG_INTRA_ONLY:
                            t.wait_ge(S["km2"].h, B["km2"] + (h + 1) * 16)
                        for qb in range(NQB):
                            idx2 = h * NQB + qb
                            if idx2 >= 2:
                                t.wait_ge(S["xp"].h, B["xp"] + idx2 - 1)
                            if hh == 0 and qb == 1:
                                t.wait_ge(S["uv"].h, B["uv"] + j * 4 + 4)
                            for ci in range(4):
                                c = qb * 4 + ci
                                ins = t.matmul(pa2[:, 1024 + (qb % 2) * 512 + ci * 128: 1024 + (qb % 2) * 512 + (ci + 1) * 128],
                                               ka[:, rot * 2048 + c * 128: rot * 2048 + c * 128 + 128],
                                               qa[:, rot * 2048 + c * 128: rot * 2048 + c * 128 + 128],
                                               start=True, stop=True)
                            ins.then_inc(S["bd"].h, 1) ; S["bd"].inc(1)
                            if DBG_TAP and h == 15 and qb == 3:
                                t.wait_ge(S["tap"].h, 16)
                            if not DBG_INTRA_ONLY:
                                if idx2 >= 1:
                                    t.wait_ge(S["xpi"].h, B["xpi"] + idx2)
                                ins = t.matmul(pa2[0:16, 2048:2560], kma[:, rot * 16:(rot + 1) * 16],
                                               qa[:, rot * 2048 + qb * 512: rot * 2048 + (qb + 1) * 512],
                                               start=True, stop=True)
                                ins.then_inc(S["bdj"].h, 1) ; S["bdj"].inc(1)
                            t.wait_ge(S["xp"].h, B["xp"] + idx2 + 1)
                            if h >= 1 and qb == 0:
                                t.wait_ge(S["oe"].h, B["oe"] + 4 * h)
                            if idx2 >= 2:
                                t.wait_ge(S["oc"].h, B["oc"] + idx2 - 1)
                            for ci in range(4):
                                c = qb * 4 + ci
                                ins = t.matmul(pa2[0:65, 2560 + (qb % 2) * 512 + ci * 128: 2560 + (qb % 2) * 512 + (ci + 1) * 128],
                                               xa[:, VA + c * 130 + hh * 65: VA + c * 130 + hh * 65 + 65],
                                               xa[:, PTI + (qb % 2) * 512 + ci * 128: PTI + (qb % 2) * 512 + (ci + 1) * 128],
                                               start=True, stop=True)
                            ins.then_inc(S["av"].h, 1) ; S["av"].inc(1)
                            if not DBG_INTRA_ONLY:
                                t.wait_ge(S["pmi"].h, B["pmi"] + idx2 + 1)
                                if idx2 >= 1:
                                    t.wait_ge(S["ocB"].h, B["ocB"] + idx2)
                                ins = t.matmul(pa2[0:65, 3584:4096], vma[0:16, (h % 2) * 65:(h % 2) * 65 + 65],
                                               xa[0:16, PTJ:PTJ + 512],
                                               start=True, stop=True)
                                ins.then_inc(S["avi"].h, 1) ; S["avi"].inc(1)
                        t.wait_ge(S["nv2"].h, B["nv2"] + h + 1)
                        for qb in range(NQB):
                            idx2 = h * NQB + qb
                            if idx2 >= 1:
                                t.wait_ge(S["oe"].h, B["oe"] + idx2)
                            ins = t.matmul(pa2[0:64, 2560 + (qb % 2) * 512: 2560 + (qb % 2) * 512 + 512],
                                           ones1[64:65, :], invb[64:65, qb * 512:(qb + 1) * 512],
                                           start=True, stop=True)
                            ins.then_inc(S["nb"].h, 1) ; S["nb"].inc(1)
                            if not DBG_INTRA_ONLY:
                                ins = t.matmul(pa2[0:64, 3584:4096],
                                               ones1[64:65, :], invb[64:65, 2048 + qb * 512:2048 + (qb + 1) * 512],
                                               start=True, stop=True)
                                ins.then_inc(S["nb"].h, 1) ; S["nb"].inc(1)
                t.wait_ge(S["woc"].h, B["woc"] + NP)
                for tc in range(16):
                    par, pb3 = (S["og"], B["og"]) if tc % 2 == 0 else (S["ogo"], B["ogo"])
                    t.wait_ge(par.h, pb3 + 16 * (tc // 2 + 1))
                    pout = pa2[:, 2048 + (tc % 2) * 1024:2048 + (tc % 2) * 1024 + 1024]
                    if tc >= 2:
                        t.wait_ge(S["oo"].h, B["oo"] + tc - 1)
                    for j2 in range(NP):
                        for oh in range(2):
                            ins = t.matmul(pout[:, oh * 512:(oh + 1) * 512],
                                           xa[:, OSC + (tc % 2) * 1024 + j2 * 128:OSC + (tc % 2) * 1024 + (j2 + 1) * 128],
                                           wob[:, j2 * 1024 + oh * 512:j2 * 1024 + (oh + 1) * 512],
                                           start=(j2 == 0), stop=(j2 == NP - 1))
                    ins.then_inc(S["mo"].h, 1) ; S["mo"].inc(1)

            @blk.vector
            def _(v):
                for j in range(NP):
                    v.wait_ge(S["vb"].h, B["vb"] + 16 + 16 * (j + 1))
                    for tc in range(16):
                        v.wait_ge(S["ve"].h, B["ve"] + j * 16 + tc + 1)
                        pv = pa2[:, (tc % 2) * 512:(tc % 2) * 512 + 128]
                        v.tensor_tensor(xa[:, VA + tc * 130: VA + tc * 130 + 64],
                                        pv[:, 0:64], vbt[:, 0:64], ALU.add)
                        ins = v.tensor_tensor(xa[:, VA + tc * 130 + 65: VA + tc * 130 + 129],
                                              pv[:, 64:128], vbt[:, 64:128], ALU.add)
                        ins.then_inc(S["vv"].h, 1) ; S["vv"].inc(1)
                    v.wait_ge(S["ke"].h, B["ke"] + j * 4 + 4)
                    if j >= 1 and not DBG_INTRA_ONLY:
                        v.wait_ge(S["km2"].h, B["km2"] + 32 * j)
                    for c in range(16):
                        v.reduce_sum(kmf[:, c:c + 1], xa[:, KP + c * 128:KP + (c + 1) * 128], axis=AXX)
                    v.drain()
                    v.tensor_scalar(kmf[:], kmf[:], 0.0078125, None, ALU.mult)
                    v.drain()
                    ins = v.tensor_copy(kmb[:], kmf[:])
                    ins.then_inc(S["kmv"].h, 1) ; S["kmv"].inc(1)
                    for q4 in range(4):
                        v.wait_ge(S["un"].h, B["un"] + j * 4 + q4 + 1)
                        ins = v.tensor_copy(rowscr[0:4, q4 * 512:(q4 + 1) * 512], pa2[0:4, 1536:2048])
                        ins.then_inc(S["uv"].h, 1) ; S["uv"].inc(1)
                    v.wait_ge(S["ug"].h, B["ug"] + j + 1)
                    v.reduce_max(ncl[0:4, 0:1], rowscr[0:4, 0:2048], axis=AXX)
                    v.drain()
                    v.nop().then_inc(S["ud"].h, 1) ; S["ud"].inc(1)
                    v.wait_ge(S["kx"].h, B["kx"] + 16 * (j + 1))
                    v.tensor_scalar(ncl[0:2, 1:2], ncl[0:2, 2:3], -1.0, None, ALU.mult)
                    v.drain()
                    v.tensor_scalar(rowscr[0:2, 0:2048], rowscr[0:2, 0:2048], ncl[0:2, 1:2],
                                    negmarg[0:2, j:j + 1], ALU.mult, ALU.add)
                    v.drain()
                    v.nop().then_inc(S["u2"].h, 1) ; S["u2"].inc(1)
                    for hh in range(2):
                        h = 2 * j + hh
                        if DBG_VMT:
                            v.wait_ge(S["vt"].h, B["vt"] + 2 * j + hh + 1)
                            if h >= 2 and not DBG_INTRA_ONLY:
                                v.wait_ge(S["avi"].h, B["avi"] + (h - 1) * NQB)
                            ins = v.tensor_copy(vma[0:16, (h % 2) * 65:(h % 2) * 65 + 65],
                                                pa2[0:16, hh * 512:hh * 512 + 65])
                            ins.then_inc(S["vte"].h, 1) ; S["vte"].inc(1)
                        for qb in range(NQB):
                            idx2 = h * NQB + qb
                            if not DBG_INTRA_ONLY:
                                v.wait_ge(S["xpi"].h, B["xpi"] + idx2 + 1)
                                ins = v.tensor_tensor(xa[0:16, PTJ:PTJ + 512], xa[0:16, PTJ:PTJ + 512],
                                                      cmaskb[0:16, qb * 512:(qb + 1) * 512], ALU.mult)
                                ins.then_inc(S["pmi"].h, 1) ; S["pmi"].inc(1)
                            v.wait_ge(S["av"].h, B["av"] + idx2 + 1)
                            if h >= 1:
                                v.wait_ge(S["oL"].h, B["oL"] + 16 * h)
                            ins = v.tensor_copy(xa[0:65, TMPA + qb * 512:TMPA + (qb + 1) * 512],
                                                pa2[0:65, 2560 + (qb % 2) * 512: 2560 + (qb % 2) * 512 + 512])
                            ins.then_inc(S["oc"].h, 1) ; S["oc"].inc(1)
                            v.wait_ge(S["oc"].h, B["oc"] + h * NQB + qb + 1)
                            if h >= 1:
                                v.wait_ge(S["nb"].h, B["nb"] + (4 if DBG_INTRA_ONLY else 8) * h)
                            with self.P.nc.allow_low_precision(reason="bf16 invden"):
                                v.reciprocal(invb[64:65, qb * 512:(qb + 1) * 512],
                                             xa[64:65, TMPA + qb * 512:TMPA + (qb + 1) * 512])
                            if not DBG_INTRA_ONLY:
                                v.wait_ge(S["avi"].h, B["avi"] + idx2 + 1)
                                ins = v.tensor_copy(xa[0:65, TMPB + qb * 512:TMPB + (qb + 1) * 512],
                                                    pa2[0:65, 3584:4096])
                                ins.then_inc(S["ocB"].h, 1) ; S["ocB"].inc(1)
                                v.wait_ge(S["ocB"].h, B["ocB"] + h * NQB + qb + 1)
                                with self.P.nc.allow_low_precision(reason="bf16 invden"):
                                    rins = v.reciprocal(invb[64:65, 2048 + qb * 512:2048 + (qb + 1) * 512],
                                                        xa[64:65, TMPB + qb * 512:TMPB + (qb + 1) * 512])
                                if qb == NQB - 1:
                                    rins.then_inc(S["nv2"].h, 1) ; S["nv2"].inc(1)
                        for qb in range(NQB):
                            idx2 = h * NQB + qb
                            NBQ = 1 if DBG_INTRA_ONLY else 2
                            slA = xa[0:64, TMPA + qb * 512:TMPA + (qb + 1) * 512]
                            slB = xa[0:64, TMPB + qb * 512:TMPB + (qb + 1) * 512]
                            v.wait_ge(S["nb"].h, B["nb"] + NBQ * idx2 + 1)
                            ins = v.tensor_tensor(slA, slA, pa2[0:64, 2560 + (qb % 2) * 512: 2560 + (qb % 2) * 512 + 512],
                                                  ALU.mult)
                            if not DBG_INTRA_ONLY:
                                v.wait_ge(S["nb"].h, B["nb"] + 2 * idx2 + 2)
                                v.tensor_tensor(slB, slB, pa2[0:64, 3584:4096], ALU.mult)
                                v.drain()
                                ins = v.tensor_tensor(slA, slA, slB, ALU.add)
                            ins.then_inc(S["oe"].h, 1) ; S["oe"].inc(1)
                for tc in range(16):
                    v.wait_ge(S["mo"].h, B["mo"] + tc + 1)
                    rsl = self.resid[:, tc * D:(tc + 1) * D]
                    v.tensor_tensor(rsl, rsl, pa2[:, 2048 + (tc % 2) * 1024:2048 + (tc % 2) * 1024 + 1024],
                                    ALU.add)
                    v.drain()
                    v.wait_ge(S["bv"].h, S["bv"].n)
                    v.tensor_tensor(rsl, rsl, bob[:], ALU.add)
                    v.drain()
                    v.nop().then_inc(S["oo"].h, 1) ; S["oo"].inc(1)

            @blk.scalar
            def _(s):
                s.wait_ge(S["bl"].h, S["bl"].n)
                for j in range(NP):
                    for wi in range(2):
                        esem, psem, pname = ((S["qe"], S["pq"], "pq") if wi == 0
                                             else (S["ke"], S["pk"], "pk"))
                        if wi == 0 and j >= 1:
                            s.wait_ge(S["ag"].h, B["ag"] + 2 * j * 48)
                        if wi == 1:
                            s.wait_ge(S["kmv"].h, B["kmv"] + j)
                        for q4 in range(4):
                            s.wait_ge(psem.h, B[pname] + j * 4 + q4 + 1)
                            dst = xa[:, (QP if wi == 0 else KP) + q4 * 512:
                                     (QP if wi == 0 else KP) + (q4 + 1) * 512]
                            if wi == 0:
                                ins = s.activation(dst, pa2[:, (q4 % 2) * 512:(q4 % 2) * 512 + 512],
                                                   AF.Identity, scale=0.125, bias=qkb[:, j:j + 1])
                            else:
                                ins = s.activation(dst, pa2[:, (q4 % 2) * 512:(q4 % 2) * 512 + 512],
                                                   AF.Identity, bias=qkb[:, 8 + j:9 + j])
                            ins.then_inc(esem.h, 1) ; esem.inc(1)
                        if wi == 1 and j > 0:
                            s.wait_ge(S["oL"].h, B["oL"] + 32 * j)
                        if wi == 0 and j > 0:
                            s.wait_ge(S["un"].h, B["un"] + j * 4)
                        s.wait_ge(esem.h, B["qe" if wi == 0 else "ke"] + j * 4 + 4)
                        ins = s.activation(xa[:, (QSQ if wi == 0 else TMPA):(QSQ if wi == 0 else TMPA) + 2048],
                                           xa[:, (QP if wi == 0 else KP):(QP if wi == 0 else KP) + 2048],
                                           AF.Square)
                        ins.then_inc(S["sq"].h, 1) ; S["sq"].inc(1)
                    s.wait_ge(S["uv"].h, B["uv"] + j * 4 + 4)
                    s.activation(rowscr[0:4, 0:2048], rowscr[0:4, 0:2048], AF.Sqrt)
                    s.drain()
                    s.nop().then_inc(S["ug"].h, 1) ; S["ug"].inc(1)
                    for hh in range(2):
                        h = 2 * j + hh
                        for qb in range(NQB):
                            idx2 = h * NQB + qb
                            s.wait_ge(S["bd"].h, B["bd"] + idx2 + 1)
                            if DBG_TAP and h == 15 and qb == 3:
                                s.wait_ge(S["tap"].h, 16)
                            if idx2 >= 2:
                                s.wait_ge(S["av"].h, B["av"] + idx2 - 1)
                            ins = s.activation(xa[:, PTI + (qb % 2) * 512: PTI + (qb % 2) * 512 + 512],
                                               pa2[:, 1024 + (qb % 2) * 512: 1024 + (qb % 2) * 512 + 512],
                                               AF.Exp)
                            ins.then_inc(S["xp"].h, 1) ; S["xp"].inc(1)
                            if not DBG_INTRA_ONLY:
                                s.wait_ge(S["bdj"].h, B["bdj"] + idx2 + 1)
                                if idx2 >= 1:
                                    s.wait_ge(S["avi"].h, B["avi"] + idx2)
                                ins = s.activation(xa[0:16, PTJ:PTJ + 512], pa2[0:16, 2048:2560], AF.Exp)
                                ins.then_inc(S["xpi"].h, 1) ; S["xpi"].inc(1)

    def _l1a_stream_mm(self, tag, W_in, wi_lo, wi_hi, moving, dsts, evac,
                       sync_cb=None, gp_cb=None, gp_tail=None, sem_out=None):
        P, nc = self.P, self.P.nc
        s_le = SB(P.sem("as_wle"))
        s_lo = SB(P.sem("as_wlo"))
        s_cv = SB(P.sem("as_wc"))
        s_mm = SB(P.sem("as_pq"))
        s_ev = SB(P.sem("as_qe"))
        if sem_out is not None:
            sem_out.append((s_mm, s_ev))
        nwi = wi_hi - wi_lo
        with nc.sbuf_tensor(P.uid("wb"), [128, 2 * 1024], BF16) as wb, \
             nc.psum_tensor(P.uid("pp"), [128, 4096], F32) as pa, \
             nc.Block(f"{tag}") as blk:
            @blk.sync
            def _(s):
                for li in range(nwi * 8):
                    if sync_cb is not None:
                        sync_cb(s, li)
            @blk.gpsimd
            def _(g):
                for li in range(nwi * 8):
                    par = s_le if li % 2 == 0 else s_lo
                    if li >= 2:
                        g.wait_ge(s_mm.h, s_mm.w(4 * (li - 1)))
                    wi, ib = wi_lo + li // 8, li % 8
                    wap = bass_rust.AP(tensor=W_in.tensor, offset=(wi * 8 + ib) * 131072,
                                       ap=[[128, 128], [16384, 8], [1, 128]])
                    g.dma_start(out=wb[:, (li % 2) * 1024:(li % 2) * 1024 + 1024],
                                in_=wap).then_inc(par.h, 16) ; par.inc(16)
                    g.wait_ge(par.h, par.n)
                    if gp_cb is not None:
                        gp_cb(g, li)
                if gp_tail is not None:
                    gp_tail(g)
            @blk.tensor
            def _(t):
                for li in range(nwi * 8):
                    t.wait_ge((s_le if li % 2 == 0 else s_lo).h,
                              (s_le if li % 2 == 0 else s_lo).w(16 * (li // 2 + 1)))
                    for q4 in range(4):
                        gi = li * 4 + q4
                        if gi >= 2:
                            t.wait_ge(s_ev.h, s_ev.w(gi - 1))
                        pp = pa[:, (gi % 2) * 512:(gi % 2) * 512 + 512]
                        for kf in range(NC_D):
                            ins = t.matmul(pp, wb[:, (li % 2) * 1024 + kf * 128:(li % 2) * 1024 + kf * 128 + 128],
                                           moving[:, kf * NTOK + q4 * 512: kf * NTOK + (q4 + 1) * 512],
                                           start=(kf == 0), stop=(kf == NC_D - 1))
                        ins.then_inc(s_mm.h, 1) ; s_mm.inc(1)
            evac(blk, pa, s_mm, s_ev, wi_lo, nwi, dsts)
        return s_ev

    def emit_l1a_tpass(self, tag, srcF, dst_d):
        P, nc = self.P, self.P.nc
        s_tp = SB(P.sem("as_pk"))
        s_ev = SB(P.sem("as_ke"))
        s_st = SB(P.sem("as_ve"))
        with nc.psum_tensor(P.uid("tps"), [128, 2048], BF16) as tp_ps, \
             nc.sbuf_tensor(P.uid("tst"), [128, 2 * 1024], BF16) as stg, \
             nc.Block(f"{tag}") as blk:
            @blk.tensor
            def _(t):
                for tc in range(16):
                    if tc >= 2:
                        t.wait_ge(s_ev.h, s_ev.w(tc - 1))
                    for ib in range(8):
                        ins = t.transpose(tp_ps[:, (tc % 2) * 1024 + ib * 128:(tc % 2) * 1024 + (ib + 1) * 128],
                                          srcF[:, ib * NTOK + tc * 128: ib * NTOK + (tc + 1) * 128],
                                          self.eye_bf[:])
                    ins.then_inc(s_tp.h, 1) ; s_tp.inc(1)
            @blk.scalar
            def _(s):
                for tc in range(16):
                    s.wait_ge(s_tp.h, s_tp.w(tc + 1))
                    if tc >= 2:
                        s.wait_ge(s_st.h, s_st.w(16 * (tc - 1)))
                    ins = s.activation(stg[:, (tc % 2) * 1024:(tc % 2) * 1024 + 1024],
                                       tp_ps[:, (tc % 2) * 1024:(tc % 2) * 1024 + 1024], AF.Identity)
                    ins.then_inc(s_ev.h, 1) ; s_ev.inc(1)
            @blk.gpsimd
            def _(g):
                for tc in range(16):
                    g.wait_ge(s_ev.h, s_ev.w(tc + 1))
                    g.dma_start(out=dst_d[tc * 128:(tc + 1) * 128, :],
                                in_=stg[:, (tc % 2) * 1024:(tc % 2) * 1024 + 1024]
                                ).then_inc(s_st.h, 16) ; s_st.inc(16)
                    g.wait_ge(s_st.h, s_st.n)

    def emit_l1a_gpass(self, tag, a_d, b_d, g_d):
        P, nc = self.P, self.P.nc
        s_la = SB(P.sem("as_vv"))
        s_lb = SB(P.sem("as_sq"))
        s_mm = SB(P.sem("as_un"))
        s_ev = SB(P.sem("as_uv"))
        s_st = SB(P.sem("as_ug"))
        with nc.sbuf_tensor(P.uid("gsa"), [128, 2 * 1024], BF16) as sa, \
             nc.sbuf_tensor(P.uid("gsb"), [128, 2 * 512], BF16) as sb, \
             nc.sbuf_tensor(P.uid("gse"), [128, 2 * 512], F32) as se, \
             nc.psum_tensor(P.uid("gps"), [128, 4096], F32) as pa, \
             nc.Block(f"{tag}") as blk:
            @blk.sync
            def _(s):
                for half in range(2):
                    for tc in range(16):
                        li = half * 16 + tc
                        par = s_la if tc % 2 == 0 else s_lb
                        if li >= 2:
                            s.wait_ge(s_mm.h, s_mm.w(8 * (li - 1)))
                        s.dma_start(out=sa[:, (tc % 2) * 1024:(tc % 2) * 1024 + 1024],
                                    in_=a_d[tc * 128:(tc + 1) * 128, :]).then_inc(par.h, 16) ; par.inc(16)
                        bap = bass_rust.AP(tensor=b_d.tensor, offset=tc * 128 * 1024 + half * 512,
                                           ap=[[1024, 128], [1, 512]])
                        s.dma_start(out=sb[:, (tc % 2) * 512:(tc % 2) * 512 + 512],
                                    in_=bap).then_inc(par.h, 16) ; par.inc(16)
            @blk.tensor
            def _(t):
                for half in range(2):
                    for tc in range(16):
                        li = half * 16 + tc
                        par = s_la if tc % 2 == 0 else s_lb
                        t.wait_ge(par.h, par.w(32 * (li // 2 + 1)))
                        if tc == 0 and half == 1:
                            t.wait_ge(s_ev.h, s_ev.w(8))
                        for jb in range(8):
                            ins = t.matmul(pa[:, jb * 512:(jb + 1) * 512],
                                           sa[:, (tc % 2) * 1024 + jb * 128:(tc % 2) * 1024 + (jb + 1) * 128],
                                           sb[:, (tc % 2) * 512:(tc % 2) * 512 + 512],
                                           start=(tc == 0), stop=(tc == 15))
                            if tc == 15:
                                ins.then_inc(s_mm.h, 1) ; s_mm.inc(1)
                        if tc != 15:
                            ins.then_inc(s_mm.h, 8) ; s_mm.inc(8)
            @blk.vector
            def _(v):
                for half in range(2):
                    for jb in range(8):
                        ei = half * 8 + jb
                        v.wait_ge(s_mm.h, s_mm.w(128 * half + 120 + jb + 1))
                        if ei >= 2:
                            v.wait_ge(s_st.h, s_st.w(16 * (ei - 1)))
                        ins = v.tensor_copy(se[:, (ei % 2) * 512:(ei % 2) * 512 + 512],
                                            pa[:, jb * 512:(jb + 1) * 512])
                        ins.then_inc(s_ev.h, 1) ; s_ev.inc(1)
            @blk.gpsimd
            def _(g):
                for half in range(2):
                    for jb in range(8):
                        ei = half * 8 + jb
                        g.wait_ge(s_ev.h, s_ev.w(ei + 1))
                        gap = bass_rust.AP(tensor=g_d.tensor, offset=jb * 128 * 1024 + half * 512,
                                           ap=[[1024, 128], [1, 512]])
                        g.dma_start(out=gap, in_=se[:, (ei % 2) * 512:(ei % 2) * 512 + 512]
                                    ).then_inc(s_st.h, 16) ; s_st.inc(16)
                        g.wait_ge(s_st.h, s_st.n)

    def emit_l1a(self, tag, Wkvq_in, W0T_in, W1T_in, W1D_in, WoT_in, wgdl_in, bgt_in,
                 groups=None):
        P, nc = self.P, self.P.nc
        laes = ExitStack()
        A1 = laes.enter_context(nc.sbuf_tensor(P.uid("laA1"), [128, NC_TOK * D], BF16))
        scr2 = laes.enter_context(nc.sbuf_tensor(P.uid("lascr2"), [128, 1024], F32))
        qd = P.dram_tmp(P.uid("laqd"), [1024, 2048], BF16)
        pre1d = P.dram_tmp(P.uid("lapre"), [1024, 2048], BF16)
        a1td = P.dram_tmp(P.uid("laa1t"), [2048, 1024], BF16)
        d2ptd = P.dram_tmp(P.uid("lad2t"), [2048, 1024], BF16)
        d1ptd = P.dram_tmp(P.uid("lad1t"), [2048, 1024], BF16)
        ktd = P.dram_tmp(P.uid("lakt"), [2048, 1024], BF16)
        g1dl = P.dram_tmp(P.uid("lag1l"), [1024, 1024], F32)
        g0dl = P.dram_tmp(P.uid("lag0l"), [1024, 1024], F32)
        g1da = P.dram_tmp(P.uid("lag1a"), [1024, 1024], F32)
        g0da = P.dram_tmp(P.uid("lag0a"), [1024, 1024], F32)
        xsdl = P.dram_tmp(P.uid("laxsl"), [1024], F32)
        xsda = P.dram_tmp(P.uid("laxsa"), [1024], F32)
        xa = self.xn
        if groups is None:
            groups = [list(range(8))]

        s_cc = SB(P.sem("as_ogo"))
        s_xs = SB(P.sem("as_bd"))
        s_xd = SB(P.sem("as_xp"))
        with nc.Block(f"{tag}p0") as blk:
            s_app = self.emit_ln(blk, "atln")
            with nc.psum_tensor(P.uid("tpl"), [128, 2048], BF16) as tp_ps:
                s_t1 = self.emit_transpose(blk, self.xn, self.xT, NC_D, tp_ps,
                                           [(s_app, s_app.n)], "att1")
            @blk.vector
            def _(v):
                v.wait_ge(s_t1.h, s_t1.n)
                for ib in range(8):
                    v.reduce_sum(self.stats[:, 120 + ib:121 + ib],
                                 self.xT[:, ib * NTOK:(ib + 1) * NTOK], axis=AXX)
                v.drain()
                v.nop().then_inc(s_xs.h, 1) ; s_xs.inc(1)
            @blk.gpsimd
            def _(g):
                g.wait_ge(s_xs.h, s_xs.n)
                xap = bass_rust.AP(tensor=xsdl.tensor, offset=0, ap=[[1, 128], [128, 8]])
                g.dma_start(out=xap, in_=self.stats[:, 120:128], allow_slow_non_contiguous=True).then_inc(s_xd.h, 16) ; s_xd.inc(16)
                g.wait_ge(s_xd.h, s_xd.n)
                g.collective_compute("AllReduce", ALU.add, replica_groups=groups,
                                     ins=[xsdl.opt()], outs=[xsda.opt()]).then_inc(s_cc.h, 1)
                s_cc.inc(1)

        def mk_proj_evac(dsts):
            def evac(blk, pa, s_mm, s_ev, wi_lo, nwi, _dsts):
                @blk.scalar
                def _(s):
                    for li in range(nwi * 8):
                        dst, ib = _dsts[li // 8], li % 8
                        for q4 in range(4):
                            gi = li * 4 + q4
                            s.wait_ge(s_mm.h, s_mm.w(gi + 1))
                            ins = s.activation(dst[:, ib * NTOK + q4 * 512: ib * NTOK + (q4 + 1) * 512],
                                               pa[:, (gi % 2) * 512:(gi % 2) * 512 + 512], AF.Identity)
                            ins.then_inc(s_ev.h, 1) ; s_ev.inc(1)
            return evac
        self._l1a_stream_mm(f"{tag}p1q", Wkvq_in, 2, 3, self.xT, [xa],
                            mk_proj_evac(None))
        s_qs = SB(P.sem("as_av"))
        with nc.Block(f"{tag}p1b") as blk:
            @blk.gpsimd
            def _(g):
                for ib in range(8):
                    g.dma_start(out=qd[ib * 128:(ib + 1) * 128, :],
                                in_=xa[:, ib * NTOK:(ib + 1) * NTOK]).then_inc(s_qs.h, 16) ; s_qs.inc(16)
                    g.wait_ge(s_qs.h, s_qs.n)
        self._l1a_stream_mm(f"{tag}p1kv", Wkvq_in, 0, 2, self.xT, [A1, xa],
                            mk_proj_evac(None))

        s_ps = SB(P.sem("as_oc"))
        s_pe = SB(P.sem("as_nv2"))
        with nc.sbuf_tensor(P.uid("lapst"), [128, 2 * 2048], BF16) as pstg:
            def evac_h1(blk, pa, s_mm, s_ev, wi_lo, nwi, dsts):
                @blk.scalar
                def _(s):
                    for li in range(8):
                        if li >= 2:
                            s.wait_ge(s_ps.h, s_ps.w(16 * (li - 1)))
                        for q4 in range(4):
                            gi = li * 4 + q4
                            s.wait_ge(s_mm.h, s_mm.w(gi + 1))
                            pp = pa[:, (gi % 2) * 512:(gi % 2) * 512 + 512]
                            col = li * NTOK + q4 * 512
                            s.activation(self.xT[:, col:col + 512], pp, AF.Silu)
                            ins = s.activation(pstg[:, (li % 2) * 2048 + q4 * 512:
                                                    (li % 2) * 2048 + (q4 + 1) * 512], pp, AF.Identity)
                            ins.then_inc(s_ev.h, 1) ; s_ev.inc(1)
            def gp_pre(g, li):
                if li >= 1:
                    lj = li - 1
                    g.wait_ge(sem_cell2[0][1].h, sem_cell2[0][1].w(lj * 4 + 4))
                    g.dma_start(out=pre1d[lj * 128:(lj + 1) * 128, :],
                                in_=pstg[:, (lj % 2) * 2048:(lj % 2) * 2048 + 2048]
                                ).then_inc(s_ps.h, 16) ; s_ps.inc(16)
                    g.wait_ge(s_ps.h, s_ps.n)
            def gp_pre_tail(g):
                g.wait_ge(sem_cell2[0][1].h, sem_cell2[0][1].w(32))
                g.dma_start(out=pre1d[7 * 128:8 * 128, :],
                            in_=pstg[:, (7 % 2) * 2048:(7 % 2) * 2048 + 2048]
                            ).then_inc(s_ps.h, 16) ; s_ps.inc(16)
                g.wait_ge(s_ps.h, s_ps.n)
            sem_cell2 = []
            self._l1a_stream_mm(f"{tag}p2a", W0T_in, 0, 1, A1, None, evac_h1,
                                gp_cb=gp_pre, gp_tail=gp_pre_tail, sem_out=sem_cell2)

        def evac_d2p(blk, pa, s_mm, s_ev, wi_lo, nwi, dsts):
            @blk.vector
            def _(v):
                for li in range(8):
                    ib = li
                    for q4 in range(4):
                        gi = li * 4 + q4
                        v.wait_ge(s_mm.h, s_mm.w(gi + 1))
                        pp = pa[:, (gi % 2) * 512:(gi % 2) * 512 + 512]
                        sl = xa[:, ib * NTOK + q4 * 512: ib * NTOK + (q4 + 1) * 512]
                        v.tensor_tensor(sl, pp, sl, ALU.subtract)
                        v.drain()
                        ins = v.tensor_scalar(sl, sl, 5.0, -5.0, ALU.min, ALU.max)
                        ins.then_inc(s_ev.h, 1) ; s_ev.inc(1)
        self._l1a_stream_mm(f"{tag}p2b", W1T_in, 0, 1, self.xT, None, evac_d2p)

        self.emit_l1a_tpass(f"{tag}t_a1", self.xT, a1td)
        self.emit_l1a_tpass(f"{tag}t_d2", xa, d2ptd)
        self.emit_l1a_gpass(f"{tag}g1", a1td, d2ptd, g1dl)
        with nc.Block(f"{tag}p4b") as blk:
            @blk.gpsimd
            def _(g):
                g.collective_compute("AllReduce", ALU.add, replica_groups=groups,
                                     ins=[g1dl.opt()], outs=[g1da.opt()]).then_inc(s_cc.h, 1)
                s_cc.inc(1)

        s_sg = SB(P.sem("as_nb"))
        with nc.sbuf_tensor(P.uid("laprl"), [128, 4 * 512], BF16) as prestg:
            sem_cell = []
            s_ple = SB(P.sem("as_oe"))
            s_plo = SB(P.sem("as_og"))
            def sync_pre(s, li):
                for q4 in range(4):
                    gi = li * 4 + q4
                    if gi >= 2:
                        s.wait_ge(sem_cell[0][1].h, sem_cell[0][1].w(gi - 1))
                    pap = bass_rust.AP(tensor=pre1d.tensor, offset=li * 128 * 2048 + q4 * 512,
                                       ap=[[2048, 128], [1, 512]])
                    par5 = s_ple if gi % 2 == 0 else s_plo
                    s.dma_start(out=prestg[:, (gi % 4) * 512:(gi % 4) * 512 + 512],
                                in_=pap).then_inc(par5.h, 16) ; par5.inc(16)
            def evac_d1p(blk, pa, s_mm, s_ev, wi_lo, nwi, dsts):
                @blk.scalar
                def _(s):
                    for li in range(8):
                        for q4 in range(4):
                            gi = li * 4 + q4
                            if gi >= 2:
                                s.wait_ge(s_ev.h, s_ev.w(gi - 1))
                            par5 = s_ple if gi % 2 == 0 else s_plo
                            s.wait_ge(par5.h, par5.w(16 * (gi // 2 + 1)))
                            ins = s.activation(self.sq_scr[:, (gi % 2) * 512:(gi % 2) * 512 + 512],
                                               prestg[:, (gi % 4) * 512:(gi % 4) * 512 + 512], AF.Sigmoid)
                            ins.then_inc(s_sg.h, 1) ; s_sg.inc(1)
                @blk.vector
                def _(v):
                    for li in range(8):
                        for q4 in range(4):
                            gi = li * 4 + q4
                            col = li * NTOK + q4 * 512
                            v.wait_ge(s_sg.h, s_sg.w(gi + 1))
                            v.wait_ge(s_mm.h, s_mm.w(gi + 1))
                            sg = self.sq_scr[:, (gi % 2) * 512:(gi % 2) * 512 + 512]
                            m1 = scr2[:, (gi % 2) * 512:(gi % 2) * 512 + 512]
                            pp = pa[:, (gi % 2) * 512:(gi % 2) * 512 + 512]
                            pr = prestg[:, (gi % 4) * 512:(gi % 4) * 512 + 512]
                            v.tensor_scalar(m1, sg, -1.0, 1.0, ALU.mult, ALU.add)
                            v.tensor_tensor(sg, pp, sg, ALU.mult)
                            v.drain()
                            v.tensor_tensor(m1, pr, m1, ALU.mult)
                            v.drain()
                            v.tensor_scalar(m1, m1, 1.0, None, ALU.add)
                            v.drain()
                            ins = v.tensor_tensor(self.xT[:, col:col + 512], sg, m1, ALU.mult)
                            ins.then_inc(s_ev.h, 1) ; s_ev.inc(1)
            self._l1a_stream_mm(f"{tag}p5", W1D_in, 0, 1, xa, None, evac_d1p,
                                sync_cb=sync_pre, sem_out=sem_cell)

        self.emit_l1a_tpass(f"{tag}t_d1", self.xT, d1ptd)
        self.emit_l1a_tpass(f"{tag}t_k", A1, ktd)
        self.emit_l1a_gpass(f"{tag}g0", ktd, d1ptd, g0dl)

        with nc.Block(f"{tag}p8") as blk:
            @blk.gpsimd
            def _(g):
                g.collective_compute("AllReduce", ALU.add, replica_groups=groups,
                                     ins=[g0dl.opt()], outs=[g0da.opt()]).then_inc(s_cc.h, 1)
                s_cc.inc(1)
                g.wait_ge(s_cc.h, s_cc.n)

        s_gl = SB(P.sem("as_oL"))
        s_gm = SB(P.sem("as_mo"))
        s_gv = SB(P.sem("as_oo"))
        s_gb = SB(P.sem("as_bl"))
        gates = laes.enter_context(nc.sbuf_tensor(P.uid("lagat"), [128, 2], F32))
        with nc.sbuf_tensor(P.uid("lawg"), [128, 16], F32) as wg, \
             nc.sbuf_tensor(P.uid("laxf"), [128, 8], F32) as xf, \
             nc.sbuf_tensor(P.uid("labg"), [1, 2], F32) as bg, \
             nc.sbuf_tensor(P.uid("lagc"), [128, 4], F32) as gc, \
             nc.sbuf_tensor(P.uid("lasg"), [1, 4], F32) as sgr, \
             nc.psum_tensor(P.uid("lagp"), [1, 2], F32) as gp_ps, \
             nc.Block(f"{tag}p8b") as blk:
            @blk.sync
            def _(s):
                s.dma_start(out=wg[:], in_=wgdl_in[:]).then_inc(s_gl.h, 16) ; s_gl.inc(16)
                s.dma_start(out=bg[:], in_=bgt_in[:]).then_inc(s_gl.h, 16) ; s_gl.inc(16)
                xap = bass_rust.AP(tensor=xsda.tensor, offset=0, ap=[[1, 128], [128, 8]])
                s.dma_start(out=xf[:], in_=xap, allow_slow_non_contiguous=True).then_inc(s_gl.h, 16) ; s_gl.inc(16)
                s.wait_ge(s_gv.h, s_gv.w(1))
                s.dma_start(out=self.scr_d[7180:7182], in_=sgr[0:1, 2:4]).then_inc(s_gb.h, 16) ; s_gb.inc(16)
                s.wait_ge(s_gb.h, s_gb.n)
                bc = bass_rust.AP(tensor=self.scr_d.tensor, offset=7180, ap=[[0, 128], [1, 2]])
                s.dma_start(out=gates[:], in_=bc).then_inc(s_gb.h, 16) ; s_gb.inc(16)
                s.wait_ge(s_gb.h, s_gb.n)
            @blk.vector
            def _(v):
                v.wait_ge(s_gl.h, s_gl.n)
                for c2 in range(2):
                    v.tensor_tensor(wg[:, c2 * 8:c2 * 8 + 8], xf[:], wg[:, c2 * 8:c2 * 8 + 8],
                                    ALU.mult)
                v.drain()
                v.reduce_sum(gc[:, 0:1], wg[:, 0:8], axis=AXX)
                v.reduce_sum(gc[:, 1:2], wg[:, 8:16], axis=AXX)
                v.drain()
                v.nop().then_inc(s_gm.h, 1) ; s_gm.inc(1)
            @blk.tensor
            def _(t):
                t.wait_ge(s_gm.h, s_gm.w(1))
                t.matmul(gp_ps[0:1, 0:1], self.ones_f32[:], gc[:, 0:1], start=True, stop=True)
                ins = t.matmul(gp_ps[0:1, 1:2], self.ones_f32[:], gc[:, 1:2], start=True, stop=True)
                ins.then_inc(s_gm.h, 1) ; s_gm.inc(1)
            @blk.vector
            def _(v):
                v.wait_ge(s_gm.h, s_gm.w(2))
                v.tensor_scalar(sgr[0:1, 0:2], gp_ps[0:1, 0:2], float(1.0 / 16384.0), None, ALU.mult)
                v.drain()
                v.tensor_tensor(sgr[0:1, 0:2], sgr[0:1, 0:2], bg[0:1, 0:2], ALU.add)
                v.drain()
                v.nop().then_inc(s_gm.h, 1) ; s_gm.inc(1)
            @blk.scalar
            def _(s):
                s.wait_ge(s_gm.h, s_gm.w(3))
                s.activation(sgr[0:1, 0:2], sgr[0:1, 0:2], AF.Sigmoid)
                s.drain()
                s.nop().then_inc(s_gm.h, 1) ; s_gm.inc(1)
            @blk.vector
            def _(v):
                v.wait_ge(s_gm.h, s_gm.w(4))
                v.tensor_scalar(sgr[0:1, 2:3], sgr[0:1, 0:1], -1.0, 1.0, ALU.mult, ALU.add)
                v.drain()
                v.tensor_scalar(sgr[0:1, 3:4], sgr[0:1, 1:2], -1.0, None, ALU.mult)
                v.drain()
                v.nop().then_inc(s_gv.h, 1) ; s_gv.inc(1)

        def upd_mm(utag, WT_in, g_ar, moving, act, dst):
            s_lw = SB(P.sem("as_ud"))
            s_up = SB(P.sem("as_kx"))
            s_mm = SB(P.sem("as_u2"))
            s_ev = SB(P.sem("as_ag"))
            with nc.sbuf_tensor(P.uid("uws"), [128, 1024], F32) as ws, \
                 nc.sbuf_tensor(P.uid("ugs"), [128, 1024], F32) as gs, \
                 nc.sbuf_tensor(P.uid("uwb"), [128, 2 * 1024], BF16) as wnb, \
                 nc.psum_tensor(P.uid("ups"), [128, 4096], F32) as pa, \
                 nc.Block(f"{utag}") as blk:
                @blk.sync
                def _(s):
                    for ib in range(8):
                        if ib >= 1:
                            s.wait_ge(s_up.h, s_up.w(ib))
                        wap = bass_rust.AP(tensor=WT_in.tensor, offset=ib * 131072,
                                           ap=[[128, 128], [16384, 8], [1, 128]])
                        s.dma_start(out=ws[:], in_=wap).then_inc(s_lw.h, 16) ; s_lw.inc(16)
                        gap = bass_rust.AP(tensor=g_ar.tensor, offset=ib * 128,
                                           ap=[[1024, 128], [131072, 8], [1, 128]])
                        s.dma_start(out=gs[:], in_=gap).then_inc(s_lw.h, 16) ; s_lw.inc(16)
                @blk.vector
                def _(v):
                    v.wait_ge(s_gv.h, s_gv.n)
                    v.wait_ge(s_gb.h, s_gb.n)
                    for ib in range(8):
                        v.wait_ge(s_lw.h, s_lw.w(32 * (ib + 1)))
                        if ib >= 2:
                            v.wait_ge(s_mm.h, s_mm.w(4 * (ib - 1)))
                        v.tensor_scalar(gs[:], gs[:], float(2.0 / 4096.0), 1.0, ALU.mult, ALU.min)
                        v.tensor_scalar(self.sq_scr[:], ws[:], gates[:, 0:1], None, ALU.mult)
                        v.drain()
                        v.tensor_scalar(gs[:], gs[:], -1.0, None, ALU.max)
                        v.drain()
                        ins = v.scalar_tensor_tensor(wnb[:, (ib % 2) * 1024:(ib % 2) * 1024 + 1024], gs[:],
                                                     gates[:, 1:2], self.sq_scr[:], ALU.mult, ALU.add)
                        ins.then_inc(s_up.h, 1) ; s_up.inc(1)
                @blk.tensor
                def _(t):
                    for ib in range(8):
                        t.wait_ge(s_up.h, s_up.w(ib + 1))
                        for q4 in range(4):
                            gi = ib * 4 + q4
                            if gi >= 2:
                                t.wait_ge(s_ev.h, s_ev.w(gi - 1))
                            pp = pa[:, (gi % 2) * 512:(gi % 2) * 512 + 512]
                            for kf in range(NC_D):
                                ins = t.matmul(pp, wnb[:, (ib % 2) * 1024 + kf * 128:(ib % 2) * 1024 + kf * 128 + 128],
                                               moving[:, kf * NTOK + q4 * 512: kf * NTOK + (q4 + 1) * 512],
                                               start=(kf == 0), stop=(kf == NC_D - 1))
                            ins.then_inc(s_mm.h, 1) ; s_mm.inc(1)
                @blk.scalar
                def _(s):
                    for ib in range(8):
                        for q4 in range(4):
                            gi = ib * 4 + q4
                            s.wait_ge(s_mm.h, s_mm.w(gi + 1))
                            ins = s.activation(dst[:, ib * NTOK + q4 * 512: ib * NTOK + (q4 + 1) * 512],
                                               pa[:, (gi % 2) * 512:(gi % 2) * 512 + 512], act)
                            ins.then_inc(s_ev.h, 1) ; s_ev.inc(1)

        s_qr = SB(P.sem("as_bv"))
        with nc.Block(f"{tag}p9a") as blk:
            @blk.sync
            def _(s):
                for ib in range(8):
                    s.dma_start(out=xa[:, ib * NTOK:(ib + 1) * NTOK],
                                in_=qd[ib * 128:(ib + 1) * 128, :]).then_inc(s_qr.h, 16) ; s_qr.inc(16)
                s.wait_ge(s_qr.h, s_qr.n)
        upd_mm(f"{tag}p9", W0T_in, g0da, xa, AF.Silu, A1)
        upd_mm(f"{tag}p10", W1T_in, g1da, A1, AF.Identity, xa)

        s_yle = SB(P.sem("as_bm"))
        s_ylo = SB(P.sem("as_bb"))
        s_yc = SB(P.sem("as_rr"))
        s_ym = SB(P.sem("as_rr2"))
        s_ye = SB(P.sem("as_mt"))
        with nc.sbuf_tensor(P.uid("yws"), [128, 2 * 1024], F32) as ws, \
             nc.sbuf_tensor(P.uid("ywb"), [128, 8 * 1024], BF16) as wob, \
             nc.psum_tensor(P.uid("yps"), [128, 4096], F32) as pa, \
             nc.Block(f"{tag}p11") as blk:
            @blk.sync
            def _(s):
                for ib in range(8):
                    par = s_yle if ib % 2 == 0 else s_ylo
                    if ib >= 2:
                        s.wait_ge(s_yc.h, s_yc.w(ib - 1))
                    s.dma_start(out=ws[:, (ib % 2) * 1024:(ib % 2) * 1024 + 1024],
                                in_=WoT_in[ib]).then_inc(par.h, 16) ; par.inc(16)
            @blk.gpsimd
            def _(g):
                for ib in range(8):
                    par = s_yle if ib % 2 == 0 else s_ylo
                    g.wait_ge(par.h, par.w(16 * (ib // 2 + 1)))
                    g.tensor_copy(wob[:, ib * 1024:(ib + 1) * 1024],
                                  ws[:, (ib % 2) * 1024:(ib % 2) * 1024 + 1024])
                    g.drain()
                    g.nop().then_inc(s_yc.h, 1) ; s_yc.inc(1)
            @blk.tensor
            def _(t):
                t.wait_ge(s_yc.h, s_yc.w(8))
                for tc in range(16):
                    if tc >= 2:
                        t.wait_ge(s_ye.h, s_ye.w(tc - 1))
                    pout = pa[:, (tc % 2) * 1024:(tc % 2) * 1024 + 1024]
                    for ib in range(8):
                        for oh in range(2):
                            ins = t.matmul(pout[:, oh * 512:(oh + 1) * 512],
                                           xa[:, ib * NTOK + tc * 128: ib * NTOK + (tc + 1) * 128],
                                           wob[:, ib * 1024 + oh * 512:ib * 1024 + (oh + 1) * 512],
                                           start=(ib == 0), stop=(ib == 7))
                    ins.then_inc(s_ym.h, 1) ; s_ym.inc(1)
            @blk.vector
            def _(v):
                for tc in range(16):
                    v.wait_ge(s_ym.h, s_ym.w(tc + 1))
                    rsl = self.resid[:, tc * D:(tc + 1) * D]
                    ins = v.tensor_tensor(rsl, rsl, pa[:, (tc % 2) * 1024:(tc % 2) * 1024 + 1024],
                                          ALU.add)
                    ins.then_inc(s_ye.h, 1) ; s_ye.inc(1)
        laes.close()


def _colblocks(Wm):
    return np.ascontiguousarray(np.stack([Wm[:, ib * 128:(ib + 1) * 128] for ib in range(8)]))


def _host_ffn_layout(w1, conv_w, conv_b, w2):
    w1b = np.ascontiguousarray(w1.T.reshape(1024, 32, 128).transpose(1, 0, 2))
    w2t = np.ascontiguousarray(w2.T)
    cw = np.ascontiguousarray(conv_w[:, 0, :].reshape(32, 128, 3).transpose(1, 0, 2).reshape(128, 96))
    cb = np.ascontiguousarray(conv_b.reshape(32, 128).T)
    return w1b, w2t, cw, cb


def _host_attn_layout(wq, wk, wv, wo):
    NP = wq.shape[1] // 128
    blocks = []
    for w in (wq, wk, wv):
        for j in range(NP):
            blocks.append(w[:, j * 128:(j + 1) * 128])
    W = np.ascontiguousarray(np.stack(blocks))
    wob = np.ascontiguousarray(wo.reshape(NP, 128, 1024))
    return W, wob


def _host_band_raw(rel_emb):
    import math
    NB, MD = 32, 128
    def bucket(n):
        if n < NB // 2:
            return n
        v = NB // 2 + int(math.log(n / (NB // 2)) / math.log(MD / (NB // 2)) * (NB - NB // 2))
        return min(v, NB - 1)
    idx = np.zeros((128, 240), np.int64)
    for kk in range(128):
        for jj in range(240):
            d = jj - kk
            idx[kk, jj] = bucket(d) if 0 <= d <= 112 else (31 if d > 112 else 0)
    return np.ascontiguousarray(rel_emb[idx, :].transpose(2, 0, 1))


def _host_l1a_layout(inp):
    Wk = np.asarray(inp["L1_attn_a_Wk"], np.float32)
    Wv = np.asarray(inp["L1_attn_a_Wv"], np.float32)
    Wq = np.asarray(inp["L1_attn_a_Wq"], np.float32)
    Wout = np.asarray(inp["L1_attn_a_Wout"], np.float32)
    W0 = np.asarray(inp["L1_attn_a_W0"], np.float32)
    W1 = np.asarray(inp["L1_attn_a_W1"], np.float32)
    Wkvq = np.concatenate([_colblocks(Wk.T), _colblocks(Wv.T), _colblocks(Wq.T)])
    wgd = np.asarray(inp["L1_attn_a_Wg_decay"], np.float32)[0]
    wgl = np.asarray(inp["L1_attn_a_Wg_lr"], np.float32)[0]
    wgdl = np.ascontiguousarray(np.concatenate([wgd.reshape(8, 128).T, wgl.reshape(8, 128).T], axis=1))
    bgt = np.array([[float(np.asarray(inp["L1_attn_a_bg_decay"], np.float32)[0]),
                     float(np.asarray(inp["L1_attn_a_bg_lr"], np.float32)[0])]], np.float32)
    return (Wkvq, _colblocks(W0.T), _colblocks(W1.T), _colblocks(W1),
            np.ascontiguousarray(Wout.T.reshape(8, 128, 1024)), wgdl, bgt)


def _build(inputs):
    inp = {k: np.asarray(v) for k, v in inputs.items()}
    x = np.asarray(inp["L0_attn_a_x"], np.float32)

    W0a, wo0a = _host_attn_layout(np.asarray(inp["L0_attn_a_Wq"], np.float32),
                                  np.asarray(inp["L0_attn_a_Wkv"], np.float32)[:, 0:512],
                                  np.asarray(inp["L0_attn_a_Wkv"], np.float32)[:, 512:1024],
                                  np.asarray(inp["L0_attn_a_Wo"], np.float32))
    band0 = _host_band_raw(np.asarray(inp["L0_attn_a_rel_emb"], np.float32))
    relv0 = np.ascontiguousarray(np.asarray(inp["L0_attn_a_rel_emb"], np.float32)[31:32, :])
    f0 = _host_ffn_layout(np.asarray(inp["L0_ffn_w1"], np.float32),
                          np.asarray(inp["L0_ffn_conv_w"], np.float32),
                          np.asarray(inp["L0_ffn_conv_b"], np.float32),
                          np.asarray(inp["L0_ffn_w2"], np.float32))
    W0b, wo0b = _host_attn_layout(np.asarray(inp["L0_attn_b_wq"], np.float32).T,
                                  np.asarray(inp["L0_attn_b_wk"], np.float32).T,
                                  np.asarray(inp["L0_attn_b_wv"], np.float32).T,
                                  np.asarray(inp["L0_attn_b_wo"], np.float32).T)
    bq0 = np.asarray(inp["L0_attn_b_bq"], np.float32)
    bk0 = np.asarray(inp["L0_attn_b_bk"], np.float32)
    qkb0 = np.ascontiguousarray(np.concatenate([bq0.reshape(8, 128).T, bk0.reshape(8, 128).T], axis=1))
    la = _host_l1a_layout(inp)
    f1 = _host_ffn_layout(np.asarray(inp["L1_ffn_w1"], np.float32),
                          np.asarray(inp["L1_ffn_conv_w"], np.float32),
                          np.asarray(inp["L1_ffn_conv_b"], np.float32),
                          np.asarray(inp["L1_ffn_w2"], np.float32))
    wq1 = np.asarray(inp["L1_attn_b_w_qkv"], np.float32)
    W1b, wo1b = _host_attn_layout(wq1[:, 0:1024], wq1[:, 1024:2048], wq1[:, 2048:3072],
                                  np.asarray(inp["L1_attn_b_w_out"], np.float32))

    P = Prog()
    kern = Kern(P)
    x_in = P.dram_in("x", [2048, 1024])
    W0a_in = P.dram_in("W0a", list(W0a.shape))
    wo0a_in = P.dram_in("wo0a", list(wo0a.shape))
    bo0a_in = P.dram_in("bo0a", [1024])
    bd0_in = P.dram_in("band0", list(band0.shape))
    rv0_in = P.dram_in("relv0", list(relv0.shape))
    f0_in = [P.dram_in(f"f0_{i}", list(a.shape)) for i, a in enumerate(f0)]
    W0b_in = P.dram_in("W0b", list(W0b.shape))
    wo0b_in = P.dram_in("wo0b", list(wo0b.shape))
    qkb0_in = P.dram_in("qkb0", [128, 16])
    bv0_in = P.dram_in("bv0", [1024])
    bo0b_in = P.dram_in("bo0b", [1024])
    la_names = ["Wkvq", "W0T", "W1T", "W1D", "WoT", "wgdl", "bgt"]
    la_in = [P.dram_in(f"la_{n}", list(a.shape)) for n, a in zip(la_names, la)]
    f1_in = [P.dram_in(f"f1_{i}", list(a.shape)) for i, a in enumerate(f1)]
    W1b_in = P.dram_in("W1b", list(W1b.shape))
    wo1b_in = P.dram_in("wo1b", list(wo1b.shape))
    bo1b_in = P.dram_in("bo1b", [1024])
    out = P.dram_out("out", [2048, 1024])

    nsc = P.nc.named_scope
    with nsc("init"):
        kern.emit_init(x_in)
    with nsc("wprep0"):
        fb0 = kern.emit_wprep_ffn("f0", *f0_in)
    with nsc("l0a"):
        kern.emit_attn_std("l0a", W0a_in, 8, wo0a_in, float(64.0 ** -0.5), causal=True,
                           band_in=bd0_in, relv_in=rv0_in, bo_in=bo0a_in)
    with nsc("ffn0a"):
        kern.emit_ffn("ffn0a", fb0)
    with nsc("l0b"):
        kern.emit_attn_l0b("l0b", W0b_in, wo0b_in, qkb0_in, bv0_in, bo0b_in)
    with nsc("ffn0b"):
        kern.emit_ffn("ffn0b", fb0)
    with nsc("wprep1"):
        fb1 = kern.emit_wprep_ffn("f1", *f1_in)
    with nsc("l1a"):
        kern.emit_l1a("la", *la_in, groups=[list(range(8))])
    with nsc("ffn1a"):
        kern.emit_ffn("ffn1a", fb1)
    with nsc("l1b"):
        kern.emit_attn_std("l1b", W1b_in, 16, wo1b_in, float(1024.0 ** -0.5), causal=False,
                           bo_in=bo1b_in)
    with nsc("ffn1b"):
        kern.emit_ffn("ffn1b", fb1)
    with nsc("fin"):
        kern.emit_fin(out)

    shared = {"W0a": W0a, "wo0a": wo0a, "bo0a": np.asarray(inp["L0_attn_a_bo"], np.float32),
              "band0": band0, "relv0": relv0,
              "W0b": W0b, "wo0b": wo0b, "qkb0": qkb0,
              "bv0": np.asarray(inp["L0_attn_b_bv"], np.float32),
              "bo0b": np.asarray(inp["L0_attn_b_bo"], np.float32),
              "W1b": W1b, "wo1b": wo1b,
              "bo1b": np.asarray(inp["L1_attn_b_b_out"], np.float32)}
    for i, a in enumerate(f0):
        shared[f"f0_{i}"] = a
    for i, a in enumerate(f1):
        shared[f"f1_{i}"] = a
    for n, a in zip(la_names, la):
        shared[f"la_{n}"] = a
    shared.update(P.host_consts)

    in_maps = [dict(shared, x=np.ascontiguousarray(x[c])) for c in range(8)]
    return P, in_maps


def kernel(**inputs):
    P, in_maps = _build(inputs)
    from concourse.bass_utils import run_bass_kernel_spmd
    res = run_bass_kernel_spmd(P.nc, in_maps, core_ids=list(range(8)))
    return np.ascontiguousarray(np.stack([res.results[c]["out"] for c in range(8)])).astype(np.float32)
